# version 09 (final)
# speedup vs baseline: 1.2278x; 1.0559x over previous
import numpy as np
import ml_dtypes
import concourse.bass as bass
import concourse.mybir as mybir
from concourse.bass_utils import run_bass_kernel_spmd

F32 = mybir.dt.float32
BF16 = mybir.dt.bfloat16
ALU = mybir.AluOpType
AF = mybir.ActivationFunctionType
AX = mybir.AxisListType

RC = float(np.float32(12582912.0))
NEG = float(np.float32(-1e30))
B = 2
HID = 2048
H_PER = 2
DH = 128
CTOK = 64
EPS = 1e-5
NCOL = 1542


class Prog:
    ENGS = ("sync", "gpsimd", "vector", "scalar", "tensor")

    def __init__(self):
        self.ops = {e: [] for e in self.ENGS}
        self.semnames = []
        self.counts = {}
        self.marks = {e: {} for e in self.ENGS}
        self.handles = {}
        self.shared = {}
        self.bank_guard = {}

    def sem(self, name):
        self.semnames.append(name)
        self.counts[name] = 0
        return name

    def op(self, eng, fn, inc=None):
        if inc is None:
            self.ops[eng].append(lambda e, fn=fn: fn(e))
            return None
        self.counts[inc] += 1
        val = self.counts[inc]
        def run(e, fn=fn, inc=inc):
            fn(e).then_inc(self.handles[inc], 1)
        self.ops[eng].append(run)
        return (inc, val)

    def dma(self, eng, sem, out, in_, **kw):
        if sem not in self.counts:
            self.sem(sem)
        self.counts[sem] += 16
        val = self.counts[sem]
        def run(e, out=out, in_=in_, kw=kw, sem=sem):
            o = out(e) if callable(out) else out
            i = in_(e) if callable(in_) else in_
            e.dma_start(out=o, in_=i, **kw).then_inc(self.handles[sem], 16)
        self.ops[eng].append(run)
        return (sem, val)

    def wait(self, eng, tok):
        if tok is None:
            return
        sem, val = tok
        if self.marks[eng].get(sem, -1) >= val:
            return
        self.marks[eng][sem] = val
        self.ops[eng].append(lambda e, sem=sem, val=val: e.wait_ge(self.handles[sem], val))

    def barrier(self, sems=None):
        sems = list(self.counts.keys()) if sems is None else sems
        for eng in self.ENGS:
            for s in sems:
                if self.counts[s] > 0:
                    self.wait(eng, (s, self.counts[s]))

    def fence(self, eng, sem):
        if sem in self.counts and self.counts[sem] > 0:
            self.wait(eng, (sem, self.counts[sem]))

    def sop(self, eng, fn):
        sem = {"vector": "DV", "scalar": "AC", "gpsimd": "GP", "tensor": "PE"}[eng]
        t = self.op(eng, fn, inc=sem)
        self.wait(eng, t)
        return t

    def bank(self, key):
        self.wait("tensor", self.bank_guard.get(key))

    def bank_done(self, key, tok):
        self.bank_guard[key] = tok


class Arena:
    def __init__(self, ap, cols, name):
        self.ap = ap; self.cols = cols; self.off = 0; self.stack = []; self.name = name; self.peak = 0

    def alloc(self, shape, name=""):
        n = 1
        for s in shape[1:]:
            n *= s
        assert self.off + n <= self.cols, f"{self.name} overflow {name}: {self.off}+{n}>{self.cols}"
        off = self.off
        self.off += (n + 15) // 16 * 16
        self.peak = max(self.peak, self.off)
        ap = self.ap[0:shape[0], off:off + n]
        if len(shape) > 2:
            pat = " ".join(f"d{i}" for i in range(1, len(shape)))
            kw = {f"d{i}": shape[i] for i in range(1, len(shape))}
            ap = ap.rearrange(f"p ({pat}) -> p {pat}", **kw)
        return ap

    def push(self):
        self.stack.append(self.off)

    def pop(self):
        self.off = self.stack.pop()


def build_program(L=8192, debug=()):
    nc = bass.Bass("TRN2", target_bir_lowering=False, dynamic_dma_scratch_size=8192)
    SEG = min(512, L)
    NSEG = L // SEG
    TT = min(256, SEG)
    NTT = SEG // TT
    NCH = SEG // CTOK
    SS = 2 * SEG
    LSL = L // 4
    FT = min(256, LSL)
    NFT = LSL // FT
    T4 = min(512, LSL)
    NT4 = LSL // T4
    MODSH = 12288 // 8
    GVSH = 16384 // 8
    OUTSH = 8192 // 8
    NHC = 128
    NOC = 64
    CW = min(512, SEG)
    NCT = SEG // CW

    din = lambda name, shape: nc.dram_tensor(name, shape, F32, kind="ExternalInput").ap()
    xT = din("xT", [HID, L])
    xsl = din("xsl", [HID, LSL])
    wproj = din("wproj", [HID, NCOL])
    convw = din("convw", [1280, 4])
    wo_in = din("wo_in", [1024, HID])
    wmod_sl = din("wmod_sl", [HID, MODSH])
    wgv_sh = din("wgv_sh", [HID, GVSH])
    wout_sh = din("wout_sh", [OUTSH, HID])
    cT = din("cT", [HID, B])
    alog = din("alog", [H_PER, 1])
    dtbi = din("dtbi", [H_PER, 1])
    onorm = din("onorm", [1, DH])
    fnorm = din("fnorm", [128, 16])
    out_x = nc.dram_tensor("out_x", [HID, LSL], F32, kind="ExternalOutput").ap()

    dr = lambda name, shape, dt=F32: nc.dram_tensor(name, shape, dt).ap()
    cc1_in = dr("cc1_in", [B, MODSH + 2])
    cc1_out = dr("cc1_out", [8, B, MODSH + 2])
    cc2_in = dr("cc2_in", [GVSH + OUTSH, HID], BF16)
    cc2_out = dr("cc2_out", [8, GVSH + OUTSH, HID], BF16)
    cc3_in = dr("cc3_in", [H_PER * DH, L], BF16)
    cc3_out = dr("cc3_out", [4, H_PER * DH, L], BF16)
    modb_d = dr("modb_d", [1, 12288])
    scl_d = dr("scl_d", [1, 16])
    bc_d = dr("bc_d", [4, 4096])
    row_d = dr("row_d", [12, SEG + 64])
    row_db = dr("row_db", [8, max(SEG, 512)], BF16)
    qs_d = dr("qs_d", [4, 2048])
    grow_d = dr("grow_d", [2, SEG])
    wsar_d = dr("wsar_d", [1, NCOL], BF16)
    x1_d = dr("x1_d", [HID, LSL], BF16)
    h2_d = dr("h2_d", [HID, LSL], BF16)
    mid_d = dr("mid_d", [8192, LSL], BF16)

    dbg = {}
    def tap(name, ap_shape, dt=F32):
        if name in debug:
            dbg[name] = nc.dram_tensor("dbg_" + name, ap_shape, dt, kind="ExternalOutput").ap()
            return dbg[name]
        return None

    p = Prog()
    LD = p.sem("LD"); LS = p.sem("LS"); LG = p.sem("LG"); ST = p.sem("ST")
    CC = p.sem("CC"); PE = p.sem("PE"); DV = p.sem("DV"); AC = p.sem("AC"); GP = p.sem("GP")

    import contextlib
    ctx = contextlib.ExitStack()
    def al(name, shape, dt):
        return ctx.enter_context(nc.sbuf_tensor(name, shape, dt))

    ident = al("ident", [128, 128], BF16)
    identf = al("identf", [128, 128], F32)
    maskE = al("maskE", [128, 128], F32)
    maskET = al("maskET", [128, 128], F32)
    maskEQ = al("maskEQ", [128, 64], F32)
    triu64 = al("triu64", [64, 64], F32)
    onesc = al("onesc", [128, 2], BF16)
    negc = al("negc", [128, 1], BF16)
    posc = al("posc", [128, 1], BF16)
    onesf = al("onesf", [128, 1], F32)
    onorm_cl = al("onorm_cl", [128, 1], F32)
    modc = al("modc", [128, 16, 8], F32)
    fnorm_c = al("fnorm_c", [128, 16], F32)
    negea = al("negea", [6, 1], F32)
    dtb_c = al("dtb_c", [6, 1], F32)
    invs_c = al("invs_c", [128, 4], F32)
    sro = al("sro", [4, 2048], F32)
    srb = al("srb", [1, 512], BF16)
    Sst = al("Sst", [128, H_PER, 128], F32)
    Sbf = al("Sbf", [128, H_PER, 128], BF16)
    wsa_c = al("wsa_c", [128, 13], F32)
    wsa_cb = al("wsa_cb", [128, 13], BF16)
    wsa_r = al("wsa_r", [1, NCOL], BF16)
    silc = al("silc", [128, 16, B], BF16)
    csb = al("csb", [128, 16, B], F32)
    smallr = al("smallr", [6, SEG], F32)
    beta4 = al("beta4", [4, SEG], F32)
    gT = al("gT", [64, 2 * NCH], F32)
    cumga = al("cumga", [64, 2 * NCH], F32)
    ctokT = al("ctokT", [64, 64], F32)
    ctok4 = al("ctok4", [4, SEG], F32)
    scr4 = al("scr4", [4, SEG], F32)
    rowb4 = al("rowb4", [4, SEG], BF16)
    rowbe = al("rowbe", [2, SEG], BF16)
    cvCbc = al("cvCbc", [128, H_PER, NCH], F32)

    ABF_COLS = 68096
    AF_COLS = 10624
    ABF = Arena(al("abf", [128, ABF_COLS], BF16)[:, :], ABF_COLS, "ABF")
    AFA = Arena(al("af", [128, AF_COLS], F32)[:, :], AF_COLS, "AF")

    pb = [ctx.enter_context(nc.psum_tensor(f"pb{i}", [128, 512], F32)) for i in range(7)]
    pbf = ctx.enter_context(nc.psum_tensor("pbf", [128, 1024], BF16))

    for name, h, pat, base, cm in (
            ("id", ident, [[-1, 128]], 0, 1),
            ("idf", identf, [[-1, 128]], 0, 1),
            ("t64", triu64, [[1, 64]], 0, -1)):
        t = p.op("gpsimd", lambda e, h=h, pat=pat, base=base, cm=cm: e.iota(
            h[:, :], pattern=pat, base=base, channel_multiplier=cm,
            allow_small_or_imprecise_dtypes=True), inc=GP)
        p.wait("vector", t)
    t = None
    for (pr, pc, base) in (((0, 64), (0, 64), -1), ((0, 64), (64, 128), -1),
                           ((64, 128), (0, 64), 0), ((64, 128), (64, 128), -1)):
        t = p.op("gpsimd", lambda e, pr=pr, pc=pc, base=base: e.iota(
            maskE[pr[0]:pr[1], pc[0]:pc[1]], pattern=[[-1, 64]], base=base,
            channel_multiplier=1, allow_small_or_imprecise_dtypes=True), inc=GP)
    for (pr, pc, base) in (((0, 64), (0, 64), -1), ((0, 64), (64, 128), 0),
                           ((64, 128), (0, 64), -1), ((64, 128), (64, 128), -1)):
        t = p.op("gpsimd", lambda e, pr=pr, pc=pc, base=base: e.iota(
            maskET[pr[0]:pr[1], pc[0]:pc[1]], pattern=[[1, 64]], base=base,
            channel_multiplier=-1, allow_small_or_imprecise_dtypes=True), inc=GP)
    for (pr, base) in (((0, 64), 0), ((64, 128), 0)):
        t = p.op("gpsimd", lambda e, pr=pr, base=base: e.iota(
            maskEQ[pr[0]:pr[1], :], pattern=[[1, 64]], base=base,
            channel_multiplier=-1, allow_small_or_imprecise_dtypes=True), inc=GP)
    p.wait("vector", t)
    t_ident = p.op("vector", lambda e: e.tensor_scalar(ident[:, :], ident[:, :], 0.0, None, ALU.is_equal), inc=DV)
    t_identf = p.op("vector", lambda e: e.tensor_scalar(identf[:, :], identf[:, :], 0.0, None, ALU.is_equal), inc=DV)
    p.op("vector", lambda e: e.tensor_scalar(maskE[:, :], maskE[:, :], 0.0, NEG, ALU.is_le, ALU.mult))
    p.op("vector", lambda e: e.tensor_scalar(maskET[:, :], maskET[:, :], 0.0, NEG, ALU.is_le, ALU.mult))
    p.op("vector", lambda e: e.tensor_scalar(maskEQ[:, :], maskEQ[:, :], 0.0, NEG, ALU.is_lt, ALU.mult))
    p.op("vector", lambda e: e.tensor_scalar(triu64[:, :], triu64[:, :], 0.0, None, ALU.is_ge))
    for h, v in ((onesc, 1.0), (negc, -1.0 / 2048.0), (posc, 1.0 / 2048.0), (onesf, 1.0)):
        p.op("vector", lambda e, h=h, v=v: e.memset(h[:, :], v), inc=DV)
    p.op("vector", lambda e: e.memset(Sst[:, :, :], 0.0), inc=DV)
    t_const = p.op("vector", lambda e: e.memset(Sbf[:, :, :], 0.0), inc=DV)

    ABF.push()
    wsb = ABF.alloc([128, 16, NCOL], "wsb")
    convd = ABF.alloc([128, 40, 128], "convd")
    AFA.push()
    wstg = AFA.alloc([128, 16, 200], "wstg")
    wstg2 = AFA.alloc([128, 16, 200], "wstg2")

    p.dma("sync", "dSML", csb[:, :, :], cT.rearrange("(c p) b -> p c b", p=128))
    p.dma("sync", "dSML", fnorm_c[:, :], fnorm)
    p.dma("sync", "dSML", negea[0:2, :], alog[:, :])
    p.dma("sync", "dSML", dtb_c[0:2, :], dtbi[:, :])
    p.dma("sync", "dSML", wstg2[:, 0:10, 0:4], convw.rearrange("(c p) k -> p c k", p=128))
    t_sml = p.dma("sync", "dSML", onorm_cl[:, :], onorm.rearrange("o d -> d o"),
                  allow_slow_non_contiguous=True)
    t_cT = t_fn = t_al = t_dt = t_cwl = t_onb = t_sml

    p.wait("scalar", t_cT)
    t_sil = p.op("scalar", lambda e: e.activation(silc[:, :, :], csb[:, :, :], AF.Silu), inc=AC)
    p.wait("scalar", t_al)
    t_ea = p.op("scalar", lambda e: e.activation(negea[0:2, :], negea[0:2, :], AF.Exp), inc=AC)
    p.wait("vector", t_ea)
    t_ea2 = p.op("vector", lambda e: e.tensor_scalar(negea[0:2, :], negea[0:2, :], -1.0, None, ALU.mult), inc=DV)

    p.wait("vector", t_cwl)
    p.wait("vector", t_ident)
    t_convd = None
    for ch in range(10):
        for j in range(4):
            t_convd = p.op("vector", lambda e, ch=ch, j=j: e.tensor_scalar(
                convd[:, ch * 4 + j, :], ident[:, :], wstg2[:, ch, j:j + 1], None, ALU.mult),
                inc=DV if (ch == 9 and j == 3) else None)

    cons = [None, t_convd]
    cgrp = [(i * 200, min(NCOL, (i + 1) * 200)) for i in range((NCOL + 199) // 200)]
    for gi, (c0, c1) in enumerate(cgrp):
        buf = wstg if gi % 2 == 0 else wstg2
        p.wait("sync", cons[gi % 2])
        tl = p.dma("sync", f"dWP{gi % 2}", buf[:, :, 0:c1 - c0],
                   wproj[:, c0:c1].rearrange("(c p) o -> p c o", p=128))
        p.wait("scalar", tl)
        tc = p.op("scalar", lambda e, buf=buf, c0=c0, c1=c1: e.activation(
            wsb[:, :, c0:c1], buf[:, :, 0:c1 - c0], AF.Identity), inc=AC)
        cons[gi % 2] = tc
    t_wsb_plain = cons[(len(cgrp) - 1) % 2]
    p.wait("sync", cons[0])
    p.wait("sync", cons[1])
    AFA.pop()
    p.barrier()

    AFA.push()
    accw = AFA.alloc([128, 2], "accw")
    mod6 = AFA.alloc([16, 6, 128], "mod6")
    red1 = AFA.alloc([128, 1], "red1")
    wt = [AFA.alloc([128, 16, 256], f"wt{i}") for i in range(2)]
    ABF.push()
    msb = ABF.alloc([128, 16, 256], "msb")
    gsc = AFA.alloc([128, 16], "gsc")
    t_aw = p.op("vector", lambda e: e.memset(accw[:, :], 0.0), inc=DV)
    p.wait("vector", t_aw)
    t_acc = t_const
    wfree = [t_const, t_const]

    def wsum(tiles, col):
        nonlocal t_acc
        for i, (src, ck) in enumerate(tiles):
            bi = i % 2
            buf = wt[bi]
            p.wait("scalar", wfree[bi])
            tl = p.dma("scalar", f"lW{bi}", buf[:, 0:ck, :], src)
            p.wait("vector", tl)
            p.wait("vector", t_acc)
            p.sop("vector", lambda e, buf=buf, ck=ck: e.reduce_sum(
                red1[:, :], buf[:, 0:ck, :], axis=AX.XY, apply_absolute_value=True))
            t = p.op("vector", lambda e, col=col: e.tensor_tensor(
                accw[:, col:col + 1], accw[:, col:col + 1], red1[:, :], ALU.add), inc=DV)
            wfree[bi] = t
            t_acc = t

    tiles = []
    for rt in range(2):
        rows = wgv_sh[rt * 1024:(rt + 1) * 1024, :].rearrange("(c p) o -> p c o", p=128)
        for ct2 in range(GVSH // 256):
            tiles.append((rows[:, :, ct2 * 256:(ct2 + 1) * 256], 8))
    wsum(tiles, 0)
    tiles = []
    rows = wout_sh.rearrange("(c p) o -> p c o", p=128)
    for ct2 in range(HID // 256):
        tiles.append((rows[:, :, ct2 * 256:(ct2 + 1) * 256], 8))
    wsum(tiles, 1)
    p.wait("tensor", t_acc)
    p.bank("b6")
    t_ws = p.op("tensor", lambda e: e.matmul(pb[6][0:1, 0:2], onesf[:, 0:1], accw[:, :],
                                             start=True, stop=True), inc=PE)
    p.wait("vector", t_ws)
    t_ws2 = p.op("vector", lambda e: e.tensor_copy(sro[0:1, 0:2], pb[6][0:1, 0:2]), inc=DV)
    p.bank_done("b6", t_ws2)

    NMT = MODSH // 256
    t_mod_ev = None
    t_msb_free = None
    mfree = [t_acc, t_acc]
    p.wait("tensor", t_sil)
    for mt in range(NMT):
        bi = mt % 2
        buf = wt[bi]
        p.wait("scalar", mfree[bi])
        tl = p.dma("scalar", f"lW{bi}", buf[:, :, :],
                   wmod_sl[:, mt * 256:(mt + 1) * 256].rearrange("(c p) o -> p c o", p=128))
        p.wait("scalar", tl)
        p.wait("scalar", t_msb_free)
        tc = p.op("scalar", lambda e, buf=buf: e.activation(msb[:, :, :], buf[:, :, :], AF.Identity), inc=AC)
        mfree[bi] = tc
        p.wait("tensor", tc)
        p.bank("b5")
        tm = None
        for kc in range(16):
            tm = p.op("tensor", lambda e, kc=kc: e.matmul(
                pb[5][0:2, 0:256], silc[:, kc, 0:2], msb[:, kc, :],
                start=(kc == 0), stop=(kc == 15)), inc=PE if kc == 15 else None)
        mfree[bi] = tm
        t_msb_free = tm
        p.wait("vector", tm)
        t_mod_ev = p.op("vector", lambda e, mt=mt: e.tensor_copy(
            sro[0:2, 320 + mt * 256:320 + (mt + 1) * 256], pb[5][0:2, 0:256]), inc=DV)
        p.bank_done("b5", t_mod_ev)

    p.wait("sync", t_mod_ev)
    p.wait("sync", t_ws2)
    t_c1a = p.dma("sync", "dC1", cc1_in[0:B, 0:MODSH], sro[0:2, 320:320 + MODSH])
    p.dma("sync", "dC1", cc1_in[0:1, MODSH:MODSH + 2], sro[0:1, 0:2])
    t_c1b = p.dma("sync", "dC1", cc1_in[1:2, MODSH:MODSH + 2], sro[0:1, 0:2])
    p.wait("gpsimd", t_c1b)
    t_ag1 = p.op("gpsimd", lambda e: e.collective_compute(
        "AllGather", ALU.bypass, replica_groups=[list(range(8))],
        ins=[cc1_in.opt()], outs=[cc1_out.opt()]), inc=CC)

    def mod_asm(e):
        pid = e.partition_id()
        r = e.alloc_register()
        e.reg_mov(r, pid)
        e.reg_alu(r, r, 2, op=ALU.arith_shift_right)
        bv = e.snap(r, min_val=0, max_val=1)
        for rr in range(8):
            e.dma_start(out=modb_d[0:1, rr * MODSH:(rr + 1) * MODSH],
                        in_=cc1_out[rr, bass.ds(bv, 1), 0:MODSH]).then_inc(p.handles["lMB"], 16)
    p.wait("scalar", t_ag1)
    if "lMB" not in p.counts:
        p.sem("lMB")
    p.counts["lMB"] += 16 * 8
    t_modb = ("lMB", p.counts["lMB"])
    p.ops["scalar"].append(mod_asm)
    p.wait("sync", t_modb)
    t_m6 = p.dma("sync", "dMO", mod6[:, :, :], modb_d.rearrange("o (j c p) -> (o c) j p", p=128, j=6))
    t_modc = None
    p.wait("tensor", t_m6)
    p.wait("tensor", t_identf)
    for j in range(6):
        p.bank("b6")
        t_tj = p.op("tensor", lambda e, j=j: e.transpose(
            pb[6][0:128, 0:16], mod6[:, j, :], identf[0:16, 0:16]), inc=PE)
        p.wait("vector", t_tj)
        t_modc = p.op("vector", lambda e, j=j: e.tensor_copy(modc[:, :, j], pb[6][0:128, 0:16]), inc=DV)
        p.bank_done("b6", t_modc)
    t_wsg = p.dma("sync", "dSC", sro[0:1, 32:48].rearrange("o (r c) -> o r c", c=2),
                  cc1_out[0:8, 0:1, MODSH:MODSH + 2].squeeze(1).unsqueeze(0))
    p.wait("vector", t_wsg)
    p.sop("vector", lambda e: e.reduce_sum(sro[0:1, 48:49], sro[0:1, 32:48].rearrange("o (r c) -> o c r", c=2)[:, 0, :], axis=AX.X))
    p.sop("vector", lambda e: e.reduce_sum(sro[0:1, 49:50], sro[0:1, 32:48].rearrange("o (r c) -> o c r", c=2)[:, 1, :], axis=AX.X))
    p.sop("vector", lambda e: e.tensor_scalar(sro[0:1, 50:51], sro[0:1, 48:49], 1.0 / (16384.0 * 2048.0), 1e-5, ALU.mult, ALU.max))
    p.sop("vector", lambda e: e.tensor_scalar(sro[0:1, 51:52], sro[0:1, 49:50], 1.0 / (8192.0 * 2048.0), 1e-5, ALU.mult, ALU.max))
    t_srcp = p.op("vector", lambda e: e.reciprocal(sro[0:1, 52:54], sro[0:1, 50:52]), inc=DV)
    p.wait("sync", t_srcp)
    t_s1 = p.dma("sync", "dSC", scl_d[0:1, 0:2], sro[0:1, 52:54])
    t_s2 = p.dma("sync", "dSC", scl_d[0:1, 2:4], sro[0:1, 50:52])
    p.fence("sync", "dSC")
    t_invs = p.dma("sync", "dSC", invs_c[:, :], scl_d[0:1, 0:4].broadcast_to([128, 4]))

    p.wait("vector", t_modc)
    p.op("vector", lambda e: e.tensor_scalar(modc[:, :, 6], modc[:, :, 1], 1.0, None, ALU.add))
    t_opc = p.op("vector", lambda e: e.tensor_scalar(modc[:, :, 7], modc[:, :, 4], 1.0, None, ALU.add), inc=DV)
    t_gab = None
    p.wait("scalar", t_modc)
    for col in (2, 5):
        p.sop("vector", lambda e, col=col: e.tensor_scalar(gsc[:, :], modc[:, :, col], 0.0, 2.0, ALU.is_ge, ALU.mult))
        tsg = p.sop("vector", lambda e: e.tensor_scalar(gsc[:, :], gsc[:, :], -1.0, None, ALU.add))
        p.wait("scalar", tsg)
        t_ab = p.op("scalar", lambda e, col=col: e.activation(modc[:, :, col], modc[:, :, col], AF.Abs), inc=AC)
        p.wait("vector", t_ab)
        p.sop("vector", lambda e, col=col: e.tensor_scalar(modc[:, :, col], modc[:, :, col], 0.25, None, ALU.max))
        t_gab = p.sop("vector", lambda e, col=col: e.tensor_tensor(modc[:, :, col], modc[:, :, col], gsc[:, :], ALU.mult))

    sabb = ABF.alloc([128, 16], "sabb")
    t_sab = p.op("vector", lambda e: e.tensor_copy(sabb[:, :], modc[:, :, 0]), inc=DV)
    ocs = [(i * 128, 128) for i in range(12)] + [(1536, 6)]
    p.wait("tensor", t_sab)
    p.wait("tensor", t_wsb_plain)
    p.bank("b6")
    t_wsa_mm = None
    for oi, (o0, ow) in enumerate(ocs):
        for kc in range(16):
            t_wsa_mm = p.op("tensor", lambda e, o0=o0, ow=ow, kc=kc, oi=oi: e.matmul(
                pb[6][0:ow, oi:oi + 1], wsb[:, kc, o0:o0 + ow], sabb[:, kc:kc + 1],
                start=(kc == 0), stop=(kc == 15)),
                inc=PE if (kc == 15 and oi == len(ocs) - 1) else None)
    p.wait("vector", t_wsa_mm)
    p.sop("vector", lambda e: e.tensor_copy(wsa_c[:, :], pb[6][:, 0:13]))
    t_wsab = p.op("vector", lambda e: e.tensor_copy(wsa_cb[:, :], wsa_c[:, :]), inc=DV)
    p.bank_done("b6", t_wsab)
    t_wsar = None
    p.wait("sync", t_wsab)
    for oi, (o0, ow) in enumerate(ocs):
        t_wsar = p.dma("sync", "dWS", wsar_d[0:1, o0:o0 + ow].rearrange("o w -> w o"), wsa_cb[0:ow, oi:oi + 1])
    p.fence("sync", "dWS")
    t_wsar = p.dma("sync", "dWS", wsa_r[0:1, 0:NCOL], wsar_d[0:1, 0:NCOL])
    p.wait("vector", t_wsa_mm)
    p.wait("vector", t_opc)
    t_wsb = None
    for kc in range(16):
        t_wsb = p.op("vector", lambda e, kc=kc: e.tensor_scalar(
            wsb[:, kc, :], wsb[:, kc, :], modc[:, kc, 6:7], None, ALU.mult),
            inc=DV if kc == 15 else None)

    trn = ABF.alloc([128, 8, 256], "trn")
    tfree = [t_gab, t_gab]
    t_trn_free = None
    t_tstores = []

    def ternarize(src_rows, n_ct, ck, inv_col, dst_rows):
        nonlocal t_trn_free
        for ct2 in range(n_ct):
            bi = ct2 % 2
            buf = wt[bi]
            p.wait("scalar", tfree[bi])
            tl = p.dma("scalar", f"lW{bi}", buf[:, 0:ck, :], src_rows[:, :, ct2 * 256:(ct2 + 1) * 256])
            p.wait("vector", tl)
            p.wait("vector", t_invs)
            p.sop("vector", lambda e, buf=buf, ck=ck, inv_col=inv_col: e.tensor_scalar(
                buf[:, 0:ck, :], buf[:, 0:ck, :], invs_c[:, inv_col:inv_col + 1], RC, ALU.mult, ALU.add))
            p.sop("vector", lambda e, buf=buf, ck=ck: e.tensor_scalar(
                buf[:, 0:ck, :], buf[:, 0:ck, :], -RC, 1.0, ALU.add, ALU.min))
            p.wait("vector", t_trn_free)
            tt = p.op("vector", lambda e, buf=buf, ck=ck: e.tensor_scalar(
                trn[:, 0:ck, :], buf[:, 0:ck, :], -1.0, None, ALU.max), inc=DV)
            tfree[bi] = tt
            p.wait("sync", tt)
            ts_ = p.dma("sync", "dC2", dst_rows[:, :, ct2 * 256:(ct2 + 1) * 256], trn[:, 0:ck, :])
            t_tstores.append(ts_)
            t_trn_free = ts_

    for rt in range(2):
        ternarize(wgv_sh[rt * 1024:(rt + 1) * 1024, :].rearrange("(c p) o -> p c o", p=128),
                  HID // 256, 8, 0,
                  cc2_in[rt * 1024:(rt + 1) * 1024, :].rearrange("(c p) o -> p c o", p=128))
    ternarize(wout_sh.rearrange("(c p) o -> p c o", p=128), HID // 256, 8, 1,
              cc2_in[GVSH:GVSH + OUTSH, :].rearrange("(c p) o -> p c o", p=128))
    for t in t_tstores:
        p.wait("gpsimd", t)
    t_ag2 = p.op("gpsimd", lambda e: e.collective_compute(
        "AllGather", ALU.bypass, replica_groups=[list(range(8))],
        ins=[cc2_in.opt()], outs=[cc2_out.opt()]), inc=CC)
    ABF.pop()
    AFA.pop()
    p.barrier()

    AFA.push()
    xst = AFA.alloc([128, 16, TT], "xst")
    osq = AFA.alloc([64, 128], "osq")
    bc1 = AFA.alloc([128, 2, TT], "bc1")
    canbc4 = AFA.alloc([128, 2, 2, 2, 64], "canbc4")
    ctbc = AFA.alloc([128, 2, 2, 128], "ctbc")
    Esc = AFA.alloc([128, 2, 128], "Esc")
    rcol = AFA.alloc([64, 64], "rcol")
    ecC = AFA.alloc([128, 2, 2, 2], "ecC")
    p3 = AFA.alloc([6, 2, 128], "p3")
    grows = AFA.alloc([2, SEG], "grows")
    percol = AFA.alloc([128, 2, 6], "percol")

    o_seg = ABF.alloc([64, 2 * NCH, 128], "o_seg")
    xb = ABF.alloc([128, 2, 16, TT], "xb")
    q_pre = ABF.alloc([128, 2, SEG + 3], "q_pre")
    k_pre = ABF.alloc([128, 4, SEG + 3], "k_pre")
    v_pre = ABF.alloc([128, 4, SEG + 3], "v_pre")
    q_hat = ABF.alloc([128, 2, SEG], "q_hat")
    k_hat = ABF.alloc([128, 2, NCH, 128], "k_hat")
    v_int = ABF.alloc([128, 2, NCH, 128], "v_int")
    gg_b = AFA.alloc([128, 2, SEG], "gg_b")
    sqs = ABF.alloc([128, 2, 512], "sqs")
    l2scr = ABF.alloc([128, 512], "l2scr")
    of_seg = ABF.alloc([128, 2, SEG], "of_seg")
    bebcc = ABF.alloc([128, 2, 2, 2, 64], "bebcc")
    eqbcc = ABF.alloc([128, 2, 2, 64], "eqbcc")
    Mt_ = ABF.alloc([128, 2, 128], "Mt")
    MTt = ABF.alloc([128, 2, 128], "MTt")
    Pw = [ABF.alloc([128, 2, 128], f"Pw{i}") for i in range(2)]
    PTw = [ABF.alloc([128, 2, 128], f"PTw{i}") for i in range(2)]
    Zw = [[ABF.alloc([128, 2, 128], f"Zw{q}{i}") for i in range(2)] for q in range(2)]
    Et = ABF.alloc([128, 2, 128], "Et")
    ETt = ABF.alloc([128, 2, 128], "ETt")
    EQt = ABF.alloc([128, 2, 64], "EQt")
    K2c = ABF.alloc([128, 2, 2, 128], "K2c")
    kbar = ABF.alloc([128, 2, 2, 128], "kbar")
    bvtm = ABF.alloc([128, 2, 2, 128], "bvtm")
    rhsu = ABF.alloc([128, 2, 128], "rhsu")
    Ut = ABF.alloc([128, 2, 128], "Ut")
    PTq = ABF.alloc([128, 2, 2, 64], "PTq")
    qtl = ABF.alloc([128, 2, 2, 64], "qtl")
    otb = AFA.alloc([64, 2, 128], "otb")
    ggt_all = ABF.alloc([64, 2 * NCH, 128], "ggt_all")

    t_xst_free = t_const
    t_rows_free = t_const
    t_grows_free = t_const
    t_xb_free = [t_const, t_const]
    t_state_ac = [None, None]
    halo_tok = None
    t_bc_free = t_const
    t_c3_prev = None
    t_ag3_stores = []
    t_pre_free = None

    for seg in range(NSEG):
        sbase = seg * SEG
        for ti in range(NTT):
            tok0 = sbase + ti * TT
            bi = ti % 2
            p.wait("sync", t_xst_free)
            t_x = p.dma("sync", "dX", xst[:, :, :],
                        xT[:, tok0:tok0 + TT].rearrange("(c p) t -> p c t", p=128))
            p.wait("scalar", t_x)
            p.wait("scalar", t_xb_free[bi])
            t_cast = p.op("scalar", lambda e, bi=bi: e.activation(
                xb[:, bi, :, :], xst[:, :, :], AF.Identity), inc=AC)
            t_xst_free = t_cast
            p.wait("tensor", t_cast)
            p.bank("b0")
            for kc in range(16):
                p.op("tensor", lambda e, bi=bi, kc=kc: e.matmul(
                    pb[0][0:1, 0:TT], negc[:, 0:1], xb[:, bi, kc, :],
                    start=(kc == 0), stop=(kc == 15)))
            p.bank("b1")
            t_mmq = None
            p.wait("scalar", t_cast)
            mm_tok = [None, None]
            for kc in range(16):
                sl = kc % 2
                p.wait("scalar", mm_tok[sl])
                t_sq = p.op("scalar", lambda e, bi=bi, kc=kc, sl=sl: e.activation(
                    sqs[:, sl, 0:TT], xb[:, bi, kc, :], AF.Square), inc=AC)
                p.wait("tensor", t_sq)
                t_mmq = p.op("tensor", lambda e, kc=kc, sl=sl: e.matmul(
                    pb[1][0:1, 0:TT], posc[:, 0:1], sqs[:, sl, 0:TT],
                    start=(kc == 0), stop=(kc == 15)), inc=PE)
                mm_tok[sl] = t_mmq
            p.wait("vector", t_mmq)
            p.sop("vector", lambda e: e.tensor_copy(sro[0:1, 0:TT], pb[0][0:1, 0:TT]))
            p.sop("vector", lambda e: e.tensor_tensor(sro[0:1, 512:512 + TT], sro[0:1, 0:TT], sro[0:1, 0:TT], ALU.mult))
            t_stat0 = p.op("vector", lambda e: e.scalar_tensor_tensor(
                sro[0:1, 512:512 + TT], pb[1][0:1, 0:TT], 1.0, sro[0:1, 512:512 + TT], ALU.mult, ALU.subtract), inc=DV)
            p.bank_done("b0", t_stat0)
            p.bank_done("b1", t_stat0)
            p.wait("vector", t_stat0)
            t_v1 = p.op("vector", lambda e: e.tensor_scalar(sro[0:1, 512:512 + TT], sro[0:1, 512:512 + TT], EPS, None, ALU.add), inc=DV)
            p.wait("scalar", t_v1)
            t_sg = p.op("scalar", lambda e: e.activation(sro[0:1, 1024:1024 + TT], sro[0:1, 512:512 + TT], AF.Sqrt), inc=AC)
            p.wait("vector", t_sg)
            p.sop("vector", lambda e: e.reciprocal(sro[0:1, 1536:1536 + TT], sro[0:1, 1024:1024 + TT]))
            t_sgb = p.op("vector", lambda e: e.tensor_copy(srb[0:1, 0:TT], sro[0:1, 1024:1024 + TT]), inc=DV)
            p.wait("sync", t_sgb)
            p.wait("sync", t_bc_free)
            t_st1 = p.dma("sync", "dB", bc_d[0:1, 0:TT], sro[0:1, 0:TT])
            t_st2 = p.dma("sync", "dB", bc_d[1:2, 0:TT], sro[0:1, 1536:1536 + TT])
            p.fence("sync", "dB")
            t_b1 = p.dma("sync", "dB", bc1[:, 0, :], bc_d[0:1, 0:TT].broadcast_to([128, TT]))
            t_b2 = p.dma("sync", "dB", bc1[:, 1, :], bc_d[1:2, 0:TT].broadcast_to([128, TT]))
            p.wait("vector", t_b2)
            t_z = p.op("vector", lambda e, bi=bi: e.tensor_tensor(
                xb[:, bi, :, :], xb[:, bi, :, :],
                bc1[:, 0:1, :].broadcast_to([128, 16, TT]), ALU.add), inc=DV)
            p.wait("tensor", t_z)
            p.wait("tensor", t_wsb)
            p.wait("tensor", t_wsar)
            p.wait("tensor", t_pre_free)
            t_pj_last = None
            t_ev = None
            for oi, (o0, ow) in enumerate(ocs):
                bk = 2 + (oi % 2)
                p.bank(f"b{bk}")
                for kc in range(16):
                    p.op("tensor", lambda e, o0=o0, ow=ow, kc=kc, bi=bi, bk=bk: e.matmul(
                        pb[bk][0:ow, 0:TT], wsb[:, kc, o0:o0 + ow], xb[:, bi, kc, :],
                        start=(kc == 0), stop=False))
                t_pj = p.op("tensor", lambda e, o0=o0, ow=ow, bk=bk: e.matmul(
                    pb[bk][0:ow, 0:TT], wsa_r[0:1, o0:o0 + ow], srb[0:1, 0:TT],
                    start=False, stop=True), inc=PE)
                t_pj_last = t_pj
                p.wait("vector", t_pj)
                toff = 3 + ti * TT
                if oi < 2:
                    dst = q_pre[:, oi, toff:toff + TT]
                elif oi < 6:
                    dst = k_pre[:, oi - 2, toff:toff + TT]
                elif oi < 10:
                    dst = v_pre[:, oi - 6, toff:toff + TT]
                elif oi < 12:
                    dst = gg_b[:, oi - 10, ti * TT:ti * TT + TT]
                else:
                    dst = smallr[0:6, ti * TT:ti * TT + TT]
                np_ = 128 if oi < 12 else 6
                t_ev = p.op("vector", lambda e, dst=dst, bk=bk, np_=np_: e.tensor_tensor(
                    dst, pb[bk][0:np_, 0:TT], bc1[0:np_, 1, :], ALU.mult), inc=DV)
                p.bank_done(f"b{bk}", t_ev)
            t_xb_free[bi] = t_pj_last
            t_bc_free = t_ev
            t_proj_done = t_ev

        if seg == 0:
            p.op("vector", lambda e: e.memset(q_pre[:, :, 0:3], 0.0), inc=DV)
            p.op("vector", lambda e: e.memset(k_pre[:, :, 0:3], 0.0), inc=DV)
            halo_tok = p.op("vector", lambda e: e.memset(v_pre[:, :, 0:3], 0.0), inc=DV)
        p.wait("tensor", t_proj_done)
        p.wait("tensor", halo_tok)
        p.wait("tensor", t_convd)
        t_sil_done = None
        t_conv_pe = None
        specs = []
        for ch in range(2):
            specs.append(("q", ch, q_pre[:, ch, :]))
        for ch in range(4):
            specs.append(("k", ch, k_pre[:, ch, :]))
        for ch in range(4):
            specs.append(("v", ch, v_pre[:, ch, :]))
        for kind, ch, src in specs:
            dch = {"q": 0, "k": 2, "v": 6}[kind] + ch
            for ct2 in range(NCT):
                c0 = ct2 * CW
                bk = 4 + ((dch * NCT + ct2) % 2)
                p.bank(f"b{bk}")
                t_cv = None
                for j in range(4):
                    t_cv = p.op("tensor", lambda e, dch=dch, j=j, src=src, c0=c0, bk=bk: e.matmul(
                        pb[bk][0:128, 0:CW], convd[:, dch * 4 + j, :], src[:, c0 + j:c0 + j + CW],
                        start=(j == 0), stop=(j == 3)), inc=PE if j == 3 else None)
                t_conv_pe = t_cv
                p.wait("scalar", t_cv)
                if kind == "q":
                    dst = q_hat[:, ch, c0:c0 + CW]
                elif kind == "k":
                    sg_, hh_ = ch // 2, ch % 2
                    dst = k_hat[:, hh_, c0 // 64:(c0 + CW) // 64, sg_ * 64:sg_ * 64 + 64]
                else:
                    sg_, hh_ = ch // 2, ch % 2
                    dst = v_int[:, hh_, c0 // 64:(c0 + CW) // 64, sg_ * 64:sg_ * 64 + 64]
                t_sil_done = p.op("scalar", lambda e, dst=dst, bk=bk: e.activation(
                    dst, pb[bk][0:128, 0:CW], AF.Silu), inc=AC)
                p.bank_done(f"b{bk}", t_sil_done)
        if seg < NSEG - 1:
            p.wait("vector", t_conv_pe)
            p.op("vector", lambda e: e.tensor_copy(q_pre[:, :, 0:3], q_pre[:, :, SEG:SEG + 3]))
            p.op("vector", lambda e: e.tensor_copy(k_pre[:, :, 0:3], k_pre[:, :, SEG:SEG + 3]))
            halo_tok = p.op("vector", lambda e: e.tensor_copy(v_pre[:, :, 0:3], v_pre[:, :, SEG:SEG + 3]), inc=DV)
            t_pre_free = t_conv_pe

        def l2pass(srcv, width, extra):
            tail = None
            W2 = min(512, width)
            for hh in range(2):
                for qt_ in range(width // W2):
                    c0 = qt_ * W2
                    blk = srcv[:, hh, c0:c0 + W2]
                    t_s = p.op("scalar", lambda e, blk=blk, W2=W2: e.activation(
                        sqs[:, 0, 0:W2], blk, AF.Square), inc=AC)
                    p.wait("tensor", t_s)
                    p.bank("b6")
                    t_m = p.op("tensor", lambda e, W2=W2: e.matmul(
                        pb[6][0:1, 0:W2], onesc[:, 0:1], sqs[:, 0, 0:W2],
                        start=True, stop=True), inc=PE)
                    p.wait("vector", t_m)
                    p.sop("vector", lambda e, W2=W2: e.tensor_copy(sro[0:1, 0:W2], pb[6][0:1, 0:W2]))
                    t_r1 = p.op("vector", lambda e, W2=W2: e.tensor_scalar(sro[0:1, 0:W2], sro[0:1, 0:W2], 1e-24, None, ALU.max), inc=DV)
                    p.bank_done("b6", t_r1)
                    p.wait("scalar", t_r1)
                    t_r2 = p.op("scalar", lambda e, W2=W2: e.activation(sro[0:1, 512:512 + W2], sro[0:1, 0:W2], AF.Sqrt), inc=AC)
                    p.wait("vector", t_r2)
                    p.sop("vector", lambda e, W2=W2: e.reciprocal(sro[0:1, 1024:1024 + W2], sro[0:1, 512:512 + W2]))
                    t_r3 = p.op("vector", lambda e, extra=extra, W2=W2: e.tensor_scalar(
                        srb[0:1, 0:W2], sro[0:1, 1024:1024 + W2], extra, None, ALU.mult), inc=DV)
                    p.wait("sync", t_r3)
                    t_r4 = p.dma("sync", "dL", row_db[6:7, 0:W2], srb[0:1, 0:W2])
                    p.fence("sync", "dL")
                    t_r5b = p.dma("sync", "dL", l2scr[:, 0:W2], row_db[6:7, 0:W2].broadcast_to([128, W2]))
                    p.wait("vector", t_r5b)
                    p.wait("vector", t_s)
                    tail = p.op("vector", lambda e, blk=blk, W2=W2: e.tensor_tensor(
                        blk, blk, l2scr[:, 0:W2], ALU.mult), inc=DV)
                    p.wait("sync", tail)
            return tail

        p.wait("scalar", t_sil_done)
        k_flat = k_hat.rearrange("p h c s -> p h (c s)")
        t_kap = l2pass(k_flat, SS, 1.0)
        t_qap = l2pass(q_hat, SEG, float(1.0 / np.sqrt(128.0)))

        p.wait("scalar", t_proj_done)
        t_b4 = p.op("scalar", lambda e: e.activation(beta4[:, :], smallr[0:4, :], AF.Sigmoid), inc=AC)
        p.wait("vector", t_b4)
        t_b42 = p.op("vector", lambda e: e.tensor_scalar(beta4[:, :], beta4[:, :], 2.0, None, ALU.mult), inc=DV)
        p.wait("sync", t_proj_done)
        p.wait("sync", t_grows_free)
        t_gr0 = p.dma("sync", "dG", grow_d[0:2, :], smallr[4:6, :])
        p.fence("sync", "dG")
        t_gl = p.dma("sync", "dG", grows[0:2, :], grow_d[0:2, :])
        p.wait("scalar", t_gl)
        t_ga = p.op("scalar", lambda e: e.activation(grows[0:2, :], grows[0:2, :], AF.Identity, bias=dtb_c[0:2, :]), inc=AC)
        p.wait("scalar", t_ga)
        t_gb2 = p.op("scalar", lambda e: e.activation(scr4[0:2, :], grows[0:2, :], AF.Abs), inc=AC)
        p.wait("scalar", t_gb2)
        t_gc = p.op("scalar", lambda e: e.activation(ctok4[0:2, :], scr4[0:2, :], AF.Exp, scale=-1.0), inc=AC)
        p.wait("vector", t_gc)
        t_gd = p.op("vector", lambda e: e.tensor_scalar(ctok4[0:2, :], ctok4[0:2, :], 1.0, None, ALU.add), inc=DV)
        p.wait("scalar", t_gd)
        t_ge2 = p.op("scalar", lambda e: e.activation(ctok4[0:2, :], ctok4[0:2, :], AF.Ln), inc=AC)
        p.wait("scalar", t_ge2)
        t_gf = p.op("scalar", lambda e: e.activation(grows[0:2, :], grows[0:2, :], AF.Relu), inc=AC)
        p.wait("vector", t_gf)
        t_g1 = p.op("vector", lambda e: e.tensor_tensor(grows[0:2, :], grows[0:2, :], ctok4[0:2, :], ALU.add), inc=DV)
        p.wait("vector", t_g1)
        p.wait("vector", t_ea2)
        t_g2 = p.op("vector", lambda e: e.tensor_scalar(grows[0:2, :], grows[0:2, :], negea[0:2, 0:1], None, ALU.mult), inc=DV)
        p.wait("sync", t_g2)
        t_gs2 = p.dma("sync", "dG", grow_d[0:2, :], grows[0:2, :])
        p.fence("sync", "dG")
        t_gT = p.dma("sync", "dG", gT[:, :], grow_d.rearrange("h (c t) -> t (h c)", t=64),
                     allow_slow_non_contiguous=True)
        p.wait("tensor", t_gT)
        p.wait("tensor", t_identf)
        p.bank("b6")
        t_cs = p.op("tensor", lambda e: e.matmul(pb[6][0:64, 0:2 * NCH], triu64[:, :], gT[:, :],
                                                 start=True, stop=True), inc=PE)
        p.wait("vector", t_cs)
        t_cum = p.op("vector", lambda e: e.tensor_copy(cumga[:, :], pb[6][0:64, 0:2 * NCH]), inc=DV)
        p.bank_done("b6", t_cum)
        p.wait("tensor", t_cum)
        p.bank("b6")
        t_ct = p.op("tensor", lambda e: e.transpose(pb[6][0:2 * NCH, 0:64], cumga[:, :], identf[0:64, 0:64]), inc=PE)
        p.wait("vector", t_ct)
        t_ctT = p.op("vector", lambda e: e.tensor_copy(ctokT[0:2 * NCH, :], pb[6][0:2 * NCH, 0:64]), inc=DV)
        p.bank_done("b6", t_ctT)
        p.wait("sync", t_ctT)
        t_rct = p.dma("sync", "dR", row_d[0:2, 0:SEG].rearrange("h (c t) -> h c t", t=64), ctokT[0:2 * NCH, :])
        t_rcC = p.dma("sync", "dR", row_d[6:7, 0:2 * NCH], ctokT[0:2 * NCH, 63:64])
        p.wait("sync", t_b42)
        t_rbt = p.dma("sync", "dR", row_d[8:12, 0:SEG], beta4[:, :])
        p.wait("sync", t_rows_free)
        p.fence("sync", "dR")
        t_lc1 = p.dma("sync", "dR", ctok4[0:2, :], row_d[0:2, 0:SEG])
        t_lc2 = p.dma("sync", "dR", ctok4[2:4, :], row_d[0:2, 0:SEG])
        p.wait("scalar", t_b42)
        t_ln = p.op("scalar", lambda e: e.activation(scr4[:, :], beta4[:, :], AF.Ln), inc=AC)
        p.wait("vector", t_ln)
        p.wait("vector", t_lc2)
        t_cal = p.op("vector", lambda e: e.tensor_tensor(scr4[:, :], scr4[:, :], ctok4[:, :], ALU.add), inc=DV)
        p.wait("sync", t_cal)
        t_rcal = p.dma("sync", "dR", row_d[2:6, 0:SEG], scr4[:, :])
        p.wait("scalar", t_rcal)
        t_ect = p.op("scalar", lambda e: e.activation(scr4[:, :], ctok4[:, :], AF.Exp), inc=AC)
        p.wait("vector", t_ect)
        t_be = p.op("vector", lambda e: e.tensor_tensor(rowb4[0:4, :], scr4[:, :], beta4[:, :], ALU.mult), inc=DV)
        t_eqr = p.op("vector", lambda e: e.tensor_copy(rowbe[0:2, :], scr4[0:2, :]), inc=DV)
        t_rows_free = t_eqr
        t_grows_free = t_gT
        p.wait("sync", t_be)
        p.wait("sync", t_eqr)
        t_r5 = p.dma("sync", "dR", row_db[0:4, 0:SEG], rowb4[0:4, :])
        t_r5b2 = p.dma("sync", "dR", row_db[4:6, 0:SEG], rowbe[0:2, :])
        p.wait("sync", t_rcC)
        p.fence("sync", "dR")
        t_cvC = p.dma("sync", "dR", cvCbc[:, :, :].rearrange("p h c -> p (h c)"),
                      row_d[6:7, 0:2 * NCH].broadcast_to([128, 2 * NCH]))

        p.wait("vector", t_kap)
        p.wait("vector", t_qap)
        p.wait("tensor", t_kap)
        p.wait("tensor", t_qap)
        t_ebc_free = [None, None]
        t_p3_free = [None, None]
        t_eC_prev = [None, None]
        t_scan_dve_last = None
        kv_v = k_hat
        vv_v = v_int
        Actx = {}

        def B_corr(c):
            X = Actx[c]
            q = c % 2
            X["t_corr"] = [None, None]
            for hh in range(2):
                p.wait("tensor", X["t_k2"][hh])
                p.wait("tensor", t_state_ac[hh])
                p.bank(f"b{4 + hh}")
                X["t_corr"][hh] = p.op("tensor", lambda e, hh=hh, q=q: e.matmul(
                    pb[4 + hh][0:128, 0:128], K2c[:, hh, q, :], Sbf[:, hh, :],
                    start=True, stop=True), inc=PE)

        def B_ru(c):
            X = Actx[c]
            X["t_ru"] = [None, None]
            for hh in range(2):
                p.wait("vector", X["t_corr"][hh])
                p.wait("vector", X["t_bv"][hh])
                X["t_ru"][hh] = p.op("vector", lambda e, hh=hh, q=c % 2: e.scalar_tensor_tensor(
                    rhsu[:, hh, :], pb[4 + hh][0:128, 0:128], -1.0, bvtm[:, hh, q, :], ALU.mult, ALU.add), inc=DV)
                p.bank_done(f"b{4 + hh}", X["t_ru"][hh])

        def B_umm(c):
            X = Actx[c]
            X["t_umm"] = [None, None]
            for hh in range(2):
                p.wait("tensor", X["t_ru"][hh])
                p.wait("tensor", X["t_zn"][hh])
                p.bank(f"b{4 + hh}")
                X["t_umm"][hh] = p.op("tensor", lambda e, hh=hh, Zc=X["Zc"][hh]: e.matmul(
                    pb[4 + hh][0:128, 128:256], Zc, rhsu[:, hh, :], start=True, stop=True), inc=PE)

        def B_U(c):
            X = Actx[c]
            X["t_U"] = [None, None]
            for hh in range(2):
                p.wait("scalar", X["t_umm"][hh])
                X["t_U"][hh] = p.op("scalar", lambda e, hh=hh: e.activation(
                    Ut[:, hh, :], pb[4 + hh][0:128, 128:256], AF.Identity), inc=AC)
                p.bank_done(f"b{4 + hh}", X["t_U"][hh])

        def B_O(c):
            X = Actx[c]
            q = c % 2
            X["t_o2"] = [None, None]
            p.bank("b2o")
            for hh in range(2):
                p.wait("tensor", X["t_qt"][hh])
                p.wait("tensor", X["t_U"][hh])
                p.op("tensor", lambda e, hh=hh, q=q: e.matmul(
                    pb[2][0:64, hh * 128:hh * 128 + 128], qtl[:, hh, q, :], Sbf[:, hh, :],
                    start=True, stop=False))
                p.wait("tensor", X["t_PT"][hh])
                X["t_o2"][hh] = p.op("tensor", lambda e, hh=hh, q=q: e.matmul(
                    pb[2][0:64, hh * 128:hh * 128 + 128], PTq[:, hh, q, :], Ut[:, hh, :],
                    start=False, stop=True), inc=PE)

        def B_oev(c):
            X = Actx[c]
            p.wait("vector", X["t_o2"][1])
            X["t_oev"] = p.op("vector", lambda e, c=c: e.tensor_copy(
                o_seg.rearrange("p (h c) t -> p h c t", h=2)[:, :, c, :],
                pb[2][0:64, 0:256].rearrange("p (h t) -> p h t", h=2)), inc=DV)
            p.bank_done("b2o", X["t_oev"])

        def B_smm(c):
            X = Actx[c]
            q = c % 2
            X["t_smm"] = [None, None]
            for hh in range(2):
                p.wait("tensor", X["t_kb"][hh])
                p.bank(f"b{4 + hh}")
                X["t_smm"][hh] = p.op("tensor", lambda e, hh=hh, q=q: e.matmul(
                    pb[4 + hh][0:128, 256:384], kbar[:, hh, q, :], Ut[:, hh, :],
                    start=True, stop=True), inc=PE)

        def B_sup(c):
            X = Actx[c]
            q = c % 2
            X["t_sup"] = [None, None]
            for hh in range(2):
                p.wait("vector", X["t_smm"][hh])
                X["t_sup"][hh] = p.op("vector", lambda e, hh=hh, q=q: e.scalar_tensor_tensor(
                    Sst[:, hh, :], Sst[:, hh, :], ecC[:, hh, q, 1:2], pb[4 + hh][0:128, 256:384],
                    ALU.mult, ALU.add), inc=DV)
                p.bank_done(f"b{4 + hh}", X["t_sup"][hh])

        def B_sbf(c):
            X = Actx[c]
            nonlocal_holder["t_scan_dve_last"] = X["t_sup"][1]
            for hh in range(2):
                p.wait("scalar", X["t_sup"][hh])
                p.wait("scalar", X["t_o2"][hh])
                t_state_ac[hh] = p.op("scalar", lambda e, hh=hh: e.activation(
                    Sbf[:, hh, :], Sst[:, hh, :], AF.Identity), inc=AC)

        nonlocal_holder = {"t_scan_dve_last": None}
        pref = {}

        def prefetch(c):
            q = c % 2
            tsl = slice(c * 64, (c + 1) * 64)
            tb = [None, None]
            for hh in range(2):
                p.wait("sync", t_ebc_free[hh])
                dE = f"dE{hh}{q}"
                p.dma("sync", dE, ctbc[:, q, hh, 0:64],
                      row_d[hh:hh + 1, tsl].broadcast_to([128, 64]))
                p.dma("sync", dE, ctbc[:, q, hh, 64:128],
                      row_d[hh:hh + 1, tsl].broadcast_to([128, 64]))
                for sg_ in range(2):
                    p.dma("sync", dE, canbc4[:, q, hh, sg_, :],
                          row_d[2 + 2 * sg_ + hh:3 + 2 * sg_ + hh, tsl].broadcast_to([128, 64]))
                    p.dma("sync", dE, bebcc[:, q, hh, sg_, :],
                          row_db[2 * sg_ + hh:2 * sg_ + hh + 1, tsl].broadcast_to([128, 64]))
                tb[hh] = p.dma("sync", dE, eqbcc[:, q, hh, :],
                               row_db[4 + hh:5 + hh, tsl].broadcast_to([128, 64]))
            p.wait("scalar", t_rcal if c == 0 else None)
            p.wait("scalar", t_p3_free[q])
            tp = None
            lp = f"lP{q}"
            for hh in range(2):
                for sg_ in range(2):
                    half = slice(sg_ * 64, sg_ * 64 + 64)
                    p.dma("scalar", lp, p3[3 * hh:3 * hh + 1, q, half], row_d[hh:hh + 1, tsl])
                    p.dma("scalar", lp, p3[3 * hh + 1:3 * hh + 2, q, half], row_d[2 + 2 * sg_ + hh:3 + 2 * sg_ + hh, tsl])
                    tp = p.dma("scalar", lp, p3[3 * hh + 2:3 * hh + 3, q, half], row_d[8 + 2 * sg_ + hh:9 + 2 * sg_ + hh, tsl])
            pref[c] = {"tb4": tb, "tp3": tp}

        Ectx = {}

        def ebuild_pc(c):
            q = c % 2
            p.wait("tensor", pref[c]["tp3"])
            p.bank("b6")
            t_tp = p.op("tensor", lambda e, q=q: e.transpose(
                pb[6][0:128, 0:6], p3[0:6, q, :], identf[0:6, 0:6]), inc=PE)
            p.wait("vector", t_tp)
            t_pc = p.op("vector", lambda e, q=q: e.tensor_copy(
                percol[:, q, :], pb[6][0:128, 0:6]), inc=DV)
            p.bank_done("b6", t_pc)
            t_p3_free[q] = t_pc
            Ectx[c] = {
                "t_pc": t_pc,
                "cvc": [percol[:, q, 3 * hh:3 * hh + 1] for hh in range(2)],
                "calc": [percol[:, q, 3 * hh + 1:3 * hh + 2] for hh in range(2)],
                "btc": [percol[:, q, 3 * hh + 2:3 * hh + 3] for hh in range(2)],
            }

        def ebuild_E(c):
            q = c % 2
            X = Ectx[c]
            for hh in range(2):
                p.wait("vector", pref[c]["tb4"][hh])
                p.wait("vector", t_eC_prev[hh])
            p.wait("vector", X["t_pc"])
            p.wait("vector", t_cvC)
            t_e1 = [None, None]
            for hh in range(2):
                p.sop("vector", lambda e, hh=hh, q=q, calc=X["calc"][hh]: e.tensor_scalar(
                    Esc[:, hh, :], ctbc[:, q, hh, :], -1.0, calc, ALU.mult, ALU.add))
            for hh in range(2):
                t_e1[hh] = p.op("vector", lambda e, hh=hh: e.tensor_tensor(
                    Esc[:, hh, :], Esc[:, hh, :], maskE[:, :], ALU.add), inc=DV)
            X["t_E"] = [None, None]
            for hh in range(2):
                p.wait("scalar", t_e1[hh])
                X["t_E"][hh] = p.op("scalar", lambda e, hh=hh: e.activation(
                    Et[:, hh, :], Esc[:, hh, :], AF.Exp), inc=AC)

        def ebuild_ET(c):
            q = c % 2
            X = Ectx[c]
            t_e2a = [None, None]
            for hh in range(2):
                p.wait("vector", X["t_E"][hh])
                p.sop("vector", lambda e, hh=hh, q=q, cvc=X["cvc"][hh]: e.tensor_scalar(
                    Esc[:, hh, :], canbc4[:, q, hh, :, :].rearrange("p s t -> p (s t)"),
                    cvc, None, ALU.subtract))
            for hh in range(2):
                t_e2a[hh] = p.op("vector", lambda e, hh=hh: e.tensor_tensor(
                    Esc[:, hh, :], Esc[:, hh, :], maskET[:, :], ALU.add), inc=DV)
            X["t_ET"] = [None, None]
            for hh in range(2):
                p.wait("scalar", t_e2a[hh])
                X["t_ET"][hh] = p.op("scalar", lambda e, hh=hh: e.activation(
                    ETt[:, hh, :], Esc[:, hh, :], AF.Exp), inc=AC)

        def ebuild_EQ(c):
            q = c % 2
            X = Ectx[c]
            t_e3 = [None, None]
            for hh in range(2):
                p.wait("vector", X["t_ET"][hh])
                p.sop("vector", lambda e, hh=hh, q=q, cvc=X["cvc"][hh]: e.tensor_scalar(
                    Esc[:, hh, 0:64], ctbc[:, q, hh, 0:64], cvc, None, ALU.subtract))
            for hh in range(2):
                t_e3[hh] = p.op("vector", lambda e, hh=hh: e.tensor_tensor(
                    Esc[:, hh, 0:64], Esc[:, hh, 0:64], maskEQ[:, :], ALU.add), inc=DV)
            X["t_EQ"] = [None, None]
            for hh in range(2):
                p.wait("scalar", t_e3[hh])
                X["t_EQ"][hh] = p.op("scalar", lambda e, hh=hh: e.activation(
                    EQt[:, hh, :], Esc[:, hh, 0:64], AF.Exp), inc=AC)

        def ebuild_eC(c):
            q = c % 2
            X = Ectx[c]
            t_e4 = [None, None]
            for hh in range(2):
                p.wait("vector", X["t_EQ"][hh])
                p.sop("vector", lambda e, hh=hh, q=q, cvc=X["cvc"][hh], c=c: e.tensor_scalar(
                    Esc[:, hh, 0:1], cvc, -1.0, cvCbc[:, hh, c:c + 1], ALU.mult, ALU.add))
                t_e4[hh] = p.op("vector", lambda e, hh=hh, c=c: e.tensor_copy(
                    Esc[:, hh, 1:2], cvCbc[:, hh, c:c + 1]), inc=DV)
            X["t_eC"] = [None, None]
            for hh in range(2):
                p.wait("scalar", t_e4[hh])
                X["t_eC"][hh] = p.op("scalar", lambda e, hh=hh, q=q: e.activation(
                    ecC[:, hh, q, :], Esc[:, hh, 0:2], AF.Exp), inc=AC)
                t_eC_prev[hh] = X["t_eC"][hh]

        prefetch(0)
        ebuild_pc(0)
        ebuild_E(0)
        ebuild_ET(0)
        ebuild_EQ(0)
        ebuild_eC(0)
        for it in range(NCH + 1):
            cB = it - 1
            if it == NCH:
                B_corr(cB); B_ru(cB); B_umm(cB); B_U(cB); B_O(cB)
                B_oev(cB); B_smm(cB); B_sup(cB); B_sbf(cB)
                break
            ci = it
            pp = ci % 2
            if ci + 1 < NCH:
                prefetch(ci + 1)
            EX = Ectx[ci]
            t_E, t_ET, t_EQ, t_eC = EX["t_E"], EX["t_ET"], EX["t_EQ"], EX["t_eC"]
            btc = EX["btc"]
            Kc = [kv_v[:, hh, ci, :] for hh in range(2)]
            Vc = [vv_v[:, hh, ci, :] for hh in range(2)]
            qc = [q_hat[:, hh, ci * 64:(ci + 1) * 64] for hh in range(2)]
            if cB >= 0:
                B_corr(cB)
            t_gr = [None, None]
            for hh in range(2):
                p.bank(f"b{hh}")
                t_gr[hh] = p.op("tensor", lambda e, Kc=Kc[hh], hh=hh: e.matmul(
                    pb[hh][0:128, 0:128], Kc, Kc, start=True, stop=True), inc=PE)
            t_M = [None, None]
            for hh in range(2):
                p.wait("vector", t_gr[hh])
                p.wait("vector", t_E[hh])
                t_M[hh] = p.op("vector", lambda e, hh=hh: e.tensor_tensor(
                    Mt_[:, hh, :], pb[hh][0:128, 0:128], Et[:, hh, :], ALU.mult), inc=DV)
            t_MT = [None, None]
            for hh in range(2):
                p.wait("vector", t_ET[hh])
                t_MT[hh] = p.op("vector", lambda e, hh=hh: e.tensor_tensor(
                    MTt[:, hh, :], pb[hh][0:128, 0:128], ETt[:, hh, :], ALU.mult), inc=DV)
                p.bank_done(f"b{hh}", t_MT[hh])
            if cB >= 0:
                B_ru(cB)
                B_umm(cB)
                B_U(cB)
                B_O(cB)
                B_oev(cB)
            t_g3 = [None, None]
            for hh in range(2):
                p.bank(f"b{hh}")
                t_g3[hh] = p.op("tensor", lambda e, Kc=Kc[hh], qc=qc[hh], hh=hh: e.matmul(
                    pb[hh][0:128, 128:192], Kc, qc, start=True, stop=True), inc=PE)
            t_PT_ = [None, None]
            for hh in range(2):
                p.wait("vector", t_g3[hh])
                p.wait("vector", t_EQ[hh])
                t_PT_[hh] = p.op("vector", lambda e, hh=hh, pp=pp: e.tensor_tensor(
                    PTq[:, hh, pp, :], pb[hh][0:128, 128:192], EQt[:, hh, :], ALU.mult), inc=DV)
                p.bank_done(f"b{hh}", t_PT_[hh])
            if cB >= 0:
                B_smm(cB)
            t_tk = [None, None]
            t_tv = [None, None]
            p.bank("pbf")
            for hh in range(2):
                p.wait("tensor", t_eC[hh])
                t_tk[hh] = p.op("tensor", lambda e, Kc=Kc[hh], hh=hh: e.transpose(
                    pbf[0:128, hh * 256:hh * 256 + 128], Kc, ident[:, :]), inc=PE)
                t_tv[hh] = p.op("tensor", lambda e, Vc=Vc[hh], hh=hh: e.transpose(
                    pbf[0:128, hh * 256 + 128:hh * 256 + 256], Vc, ident[:, :]), inc=PE)
            t_kb = [None, None]
            t_bv = [None, None]
            for hh in range(2):
                p.wait("vector", t_tv[1])
                t_kb[hh] = p.op("vector", lambda e, hh=hh, pp=pp: e.tensor_scalar(
                    kbar[:, hh, pp, :], pbf[0:128, hh * 256:hh * 256 + 128], ecC[:, hh, pp, 0:1], None, ALU.mult), inc=DV)
                t_bv[hh] = p.op("vector", lambda e, hh=hh, pp=pp, btc=btc[hh]: e.tensor_scalar(
                    bvtm[:, hh, pp, :], pbf[0:128, hh * 256 + 128:hh * 256 + 256], btc, None, ALU.mult), inc=DV)
            p.bank_done("pbf", t_bv[1])
            t_k2 = [None, None]
            t_qt = [None, None]
            for hh in range(2):
                t_k2[hh] = p.op("vector", lambda e, Kc=Kc[hh], hh=hh, pp=pp: e.tensor_tensor(
                    K2c[:, hh, pp, :].rearrange("p (s t) -> p s t", s=2),
                    Kc.rearrange("p (s t) -> p s t", s=2),
                    bebcc[:, pp, hh, :, :], ALU.mult), inc=DV)
                t_qt[hh] = p.op("vector", lambda e, qc=qc[hh], hh=hh, pp=pp: e.tensor_tensor(
                    qtl[:, hh, pp, :], qc, eqbcc[:, pp, hh, :], ALU.mult), inc=DV)
                t_ebc_free[hh] = t_qt[hh]
            if cB >= 0:
                B_sup(cB)
                B_sbf(cB)
            t_zn = [None, None]
            Zc = [None, None]
            curP = [Mt_[:, 0, :], Mt_[:, 1, :]]
            curPT = [MTt[:, 0, :], MTt[:, 1, :]]
            for hh in range(2):
                p.wait("vector", t_MT[hh])
                Zc[hh] = Zw[pp][0][:, hh, :]
                t_zn[hh] = p.op("vector", lambda e, hh=hh, Zc=Zc[hh]: e.tensor_tensor(
                    Zc, ident[:, :], MTt[:, hh, :], ALU.subtract), inc=DV)
            zi = [0, 0]
            nxt = ci + 1 if ci + 1 < NCH else None
            for lvl in range(6):
                t_p2 = [None, None]
                for hh in range(2):
                    if lvl == 0:
                        p.wait("tensor", t_M[hh])
                        p.wait("tensor", t_MT[hh])
                    p.bank(f"b{hh}")
                    t_p2[hh] = p.op("tensor", lambda e, curP=curP[hh], curPT=curPT[hh], hh=hh: e.matmul(
                        pb[hh][0:128, 256:384], curPT, curP, start=True, stop=True), inc=PE)
                newP = [Pw[lvl % 2][:, hh, :] for hh in range(2)]
                t_np = [None, None]
                for hh in range(2):
                    p.wait("vector", t_p2[hh])
                    t_np[hh] = p.op("vector", lambda e, newP=newP[hh], hh=hh: e.tensor_copy(
                        newP, pb[hh][0:128, 256:384]), inc=DV)
                    p.bank_done(f"b{hh}", t_np[hh])
                if lvl < 5:
                    t_pt2 = [None, None]
                    for hh in range(2):
                        p.bank(f"b{hh}")
                        t_pt2[hh] = p.op("tensor", lambda e, curP=curP[hh], curPT=curPT[hh], hh=hh: e.matmul(
                            pb[hh][0:128, 384:512], curP, curPT, start=True, stop=True), inc=PE)
                    newPT = [PTw[lvl % 2][:, hh, :] for hh in range(2)]
                    for hh in range(2):
                        p.wait("scalar", t_pt2[hh])
                        t_npt = p.op("scalar", lambda e, newPT=newPT[hh], hh=hh: e.activation(
                            newPT, pb[hh][0:128, 384:512], AF.Identity), inc=AC)
                        p.bank_done(f"b{hh}", t_npt)
                        p.wait("tensor", t_npt)
                        curPT[hh] = newPT[hh]
                t_zm = [None, None]
                for hh in range(2):
                    curP[hh] = newP[hh]
                    p.wait("tensor", t_zn[hh])
                    p.wait("tensor", t_np[hh])
                    p.bank(f"b{hh}")
                    t_zm[hh] = p.op("tensor", lambda e, newP=newP[hh], Zc=Zc[hh], hh=hh: e.matmul(
                        pb[hh][0:128, 256:384], newP, Zc, start=True, stop=True), inc=PE)
                for hh in range(2):
                    Zn = Zw[pp][1 - zi[hh]][:, hh, :]
                    p.wait("vector", t_zm[hh])
                    t_zn[hh] = p.op("vector", lambda e, Zn=Zn, Zc=Zc[hh], hh=hh: e.tensor_tensor(
                        Zn, pb[hh][0:128, 256:384], Zc, ALU.add), inc=DV)
                    p.bank_done(f"b{hh}", t_zn[hh])
                    Zc[hh] = Zn
                    zi[hh] = 1 - zi[hh]
                if nxt is not None:
                    if lvl == 0:
                        ebuild_pc(nxt)
                    elif lvl == 1:
                        ebuild_E(nxt)
                    elif lvl == 2:
                        ebuild_ET(nxt)
                    elif lvl == 3:
                        ebuild_EQ(nxt)
                    elif lvl == 4:
                        ebuild_eC(nxt)
            Actx[ci] = {"t_k2": t_k2, "t_qt": t_qt, "t_PT": t_PT_, "t_zn": list(t_zn),
                        "Zc": list(Zc), "t_kb": t_kb, "t_bv": t_bv}

        t_scan_dve_last = nonlocal_holder["t_scan_dve_last"]


        p.wait("vector", t_c3_prev)
        p.wait("scalar", t_scan_dve_last)
        t_ep_last = None
        units = [(ci, hh) for ci in range(NCH) for hh in range(2)]
        t_gs = [None] * len(units)
        for k, (ci, hh) in enumerate(units):
            bk, px, c0 = ("b5", pb[5], 0) if k % 2 == 0 else ("b6", pb[6], 0)
            p.bank(bk)
            t_tg = p.op("tensor", lambda e, hh=hh, ci=ci, px=px, c0=c0: e.transpose(
                px[0:64, c0:c0 + 128], gg_b[:, hh, ci * 64:(ci + 1) * 64], identf[0:128, 0:128]), inc=PE)
            p.wait("scalar", t_tg)
            t_gs[k] = p.op("scalar", lambda e, k=k, px=px, c0=c0: e.activation(
                ggt_all[:, k, :], px[0:64, c0:c0 + 128], AF.Silu), inc=AC)
            p.bank_done(bk, t_gs[k])
        t_sq2 = None
        for k, (ci, hh) in enumerate(units):
            p.wait("scalar", t_sq2)
            t_sq2 = p.op("scalar", lambda e, k=k, ci=ci, hh=hh: e.activation(
                osq[:, :], o_seg[:, hh * NCH + ci, :], AF.Square, accum_out=rcol[0:64, k:k + 1]), inc=AC)
        NU = len(units)
        p.wait("vector", t_sq2)
        t_rs = p.op("vector", lambda e, NU=NU: e.tensor_scalar(
            rcol[0:64, 16:16 + NU], rcol[0:64, 0:NU], 1.0 / 128.0, EPS, ALU.mult, ALU.add), inc=DV)
        p.wait("scalar", t_rs)
        t_rq = p.op("scalar", lambda e, NU=NU: e.activation(rcol[0:64, 32:32 + NU], rcol[0:64, 16:16 + NU], AF.Sqrt), inc=AC)
        p.wait("vector", t_rq)
        t_ri = p.op("vector", lambda e, NU=NU: e.reciprocal(rcol[0:64, 48:48 + NU], rcol[0:64, 32:32 + NU]), inc=DV)
        p.wait("vector", t_ri)
        t_otb_free = [None, None]
        for k, (ci, hh) in enumerate(units):
            sl = k % 2
            p.wait("vector", t_gs[k])
            p.wait("vector", t_otb_free[sl])
            t_ob = p.op("vector", lambda e, k=k, ci=ci, hh=hh, sl=sl: e.scalar_tensor_tensor(
                otb[0:64, sl, :], o_seg[:, hh * NCH + ci, :], rcol[0:64, 48 + k:49 + k],
                ggt_all[:, k, :], ALU.mult, ALU.mult), inc=DV)
            bk, px, c0 = ("b5", pb[5], 0) if k % 2 == 0 else ("b6", pb[6], 0)
            p.wait("tensor", t_ob)
            p.bank(bk)
            t_tt = p.op("tensor", lambda e, sl=sl, px=px, c0=c0: e.transpose(
                px[0:128, c0:c0 + 64], otb[0:64, sl, :], identf[0:64, 0:64]), inc=PE)
            t_otb_free[sl] = t_tt
            p.wait("vector", t_tt)
            t_ep_last = p.op("vector", lambda e, hh=hh, ci=ci, px=px, c0=c0: e.tensor_scalar(
                of_seg[:, hh, ci * 64:(ci + 1) * 64], px[0:128, c0:c0 + 64], onorm_cl[:, 0:1],
                None, ALU.mult), inc=DV)
            p.bank_done(bk, t_ep_last)
        p.wait("sync", t_ep_last)
        for hh in range(2):
            t_c3 = p.dma("sync", "dC3", cc3_in[hh * 128:(hh + 1) * 128, sbase:sbase + SEG], of_seg[:, hh, :])
            t_ag3_stores.append(t_c3)
        t_c3_prev = t_ag3_stores[-1]
        p.barrier()

    for t in t_ag3_stores:
        p.wait("gpsimd", t)
    t_ag3 = p.op("gpsimd", lambda e: e.collective_compute(
        "AllGather", ALU.bypass, replica_groups=[[0, 1, 2, 3], [4, 5, 6, 7]],
        ins=[cc3_in.opt()], outs=[cc3_out.opt()]), inc=CC)
    AFA.pop()
    ABF.pop()
    p.barrier()

    qbc2 = AFA.alloc([128, 2, LSL], "qbc2")
    R2x = AFA.alloc([128, LSL], "R2x")
    ABF.push()
    aqf = ABF.alloc([128, 16, LSL], "aqf")
    AFA.push()
    ABF.push()
    wo_sb = ABF.alloc([128, 8, HID], "wo_sb")
    ofl = ABF.alloc([128, 8, FT], "ofl")
    x1b = ABF.alloc([128, 16, FT], "x1b")
    h2b = ABF.alloc([128, 16, FT], "h2b")
    sq2 = ABF.alloc([128, 16, FT], "sq2")
    xslb = ABF.alloc([128, 16, FT], "xslb")
    x1f = AFA.alloc([128, 16, FT // 2], "x1f")
    Rmx = AFA.alloc([128, FT], "Rmx")
    qbc = AFA.alloc([128, 2, FT], "qbc")
    qtm = AFA.alloc([128, 8], "qtm")
    bcB = AFA.alloc([128, 2, FT], "bcB")
    wof = AFA.alloc([128, 8, 128], "wof")

    t_wo_c = None
    for ct2 in range(HID // 128):
        p.wait("scalar", t_wo_c)
        t_lw = p.dma("scalar", "lWO", wof[:, :, :],
                     wo_in[:, ct2 * 128:(ct2 + 1) * 128].rearrange("(c p) o -> p c o", p=128))
        p.wait("scalar", t_lw)
        t_wo_c = p.op("scalar", lambda e, ct2=ct2: e.activation(
            wo_sb[:, :, ct2 * 128:(ct2 + 1) * 128], wof[:, :, :], AF.Identity), inc=AC)

    t_ofl_free = None
    t_bcB_free = None
    t_x1_store = []
    t_aq_done = []
    t_x1b_free = []
    for ft in range(NFT):
        tok0 = ft * FT
        def og(e, ft=ft):
            off = p.shared.get("hgoff")
            if off is None:
                pid = e.partition_id()
                r = e.alloc_register()
                e.reg_mov(r, pid)
                e.reg_alu(r, r, 4, op=ALU.mod)
                e.reg_alu(r, r, LSL, op=ALU.mult)
                off = e.snap(r, min_val=0, max_val=3 * LSL)
                p.shared["hgoff"] = off
            base = cc3_out.rearrange("r (h p) t -> p (r h) t", p=128)[:, :, ft * FT:]
            e.dma_start(out=ofl[:, :, :],
                        in_=base[:, :, bass.ds(off, FT)]).then_inc(p.handles["lOG"], 16)
        p.wait("scalar", t_ag3)
        p.wait("scalar", t_ofl_free)
        if "lOG" not in p.counts:
            p.sem("lOG")
        p.counts["lOG"] += 16
        t_og = ("lOG", p.counts["lOG"])
        p.ops["scalar"].append(og)
        t_xs = None
        for hf in range(2):
            txl = p.dma("sync", "dX", x1f[:, :, :],
                        xsl[:, tok0 + hf * (FT // 2):tok0 + (hf + 1) * (FT // 2)].rearrange("(c p) t -> p c t", p=128))
            p.wait("scalar", txl)
            p.wait("scalar", t_xs)
            t_xs = p.op("scalar", lambda e, hf=hf: e.activation(
                xslb[:, :, hf * (FT // 2):(hf + 1) * (FT // 2)], x1f[:, :, :], AF.Identity), inc=AC)
            p.wait("sync", t_xs)
        p.wait("tensor", t_og)
        p.wait("tensor", t_wo_c)
        t_x1 = None
        t_wo_pe = None
        for fc in range(16):
            bk = fc % 2
            p.bank(f"b{bk}")
            for kc in range(8):
                t_wo_pe = p.op("tensor", lambda e, fc=fc, kc=kc, bk=bk: e.matmul(
                    pb[bk][0:128, 0:FT], wo_sb[:, kc, fc * 128:(fc + 1) * 128], ofl[:, kc, :],
                    start=(kc == 0), stop=(kc == 7)), inc=PE if kc == 7 else None)
            p.wait("vector", t_wo_pe)
            p.wait("vector", t_xs)
            for tkk in t_x1b_free:
                p.wait("vector", tkk)
            t_x1 = p.op("vector", lambda e, fc=fc, bk=bk: e.scalar_tensor_tensor(
                x1b[:, fc, :], pb[bk][0:128, 0:FT], modc[:, fc, 2:3], xslb[:, fc, :], ALU.mult, ALU.add), inc=DV)
            p.bank_done(f"b{bk}", t_x1)
        t_ofl_free = t_wo_pe
        t_x1b = t_x1
        p.wait("sync", t_x1b)
        t_x1s = p.dma("sync", "dSP", x1_d[:, tok0:tok0 + FT].rearrange("(c p) t -> p c t", p=128), x1b[:, :, :])
        t_x1_store.append(t_x1s)
        p.wait("tensor", t_x1b)
        p.bank("b2")
        for kc in range(16):
            p.op("tensor", lambda e, kc=kc: e.matmul(
                pb[2][0:1, 0:FT], negc[:, 0:1], x1b[:, kc, :], start=(kc == 0), stop=(kc == 15)))
        p.bank("b3")
        t_q2m = None
        p.wait("scalar", t_x1b)
        for kc in range(16):
            t_sq = p.op("scalar", lambda e, kc=kc: e.activation(
                sq2[:, kc, :], x1b[:, kc, :], AF.Square), inc=AC)
            p.wait("tensor", t_sq)
            t_q2m = p.op("tensor", lambda e, kc=kc: e.matmul(
                pb[3][0:1, 0:FT], posc[:, 0:1], sq2[:, kc, :], start=(kc == 0), stop=(kc == 15)),
                inc=PE if kc == 15 else None)
        p.wait("vector", t_q2m)
        p.sop("vector", lambda e: e.tensor_copy(sro[0:1, 0:FT], pb[2][0:1, 0:FT]))
        p.sop("vector", lambda e: e.tensor_tensor(sro[0:1, 512:512 + FT], sro[0:1, 0:FT], sro[0:1, 0:FT], ALU.mult))
        t_ln2s = p.op("vector", lambda e: e.scalar_tensor_tensor(
            sro[0:1, 512:512 + FT], pb[3][0:1, 0:FT], 1.0, sro[0:1, 512:512 + FT], ALU.mult, ALU.subtract), inc=DV)
        p.bank_done("b2", t_ln2s)
        p.bank_done("b3", t_ln2s)
        p.wait("vector", t_ln2s)
        t_v2 = p.op("vector", lambda e: e.tensor_scalar(sro[0:1, 512:512 + FT], sro[0:1, 512:512 + FT], EPS, None, ALU.add), inc=DV)
        p.wait("scalar", t_v2)
        t_sg2 = p.op("scalar", lambda e: e.activation(sro[0:1, 1024:1024 + FT], sro[0:1, 512:512 + FT], AF.Sqrt), inc=AC)
        p.wait("vector", t_sg2)
        t_iv2 = p.op("vector", lambda e: e.reciprocal(sro[0:1, 1024:1024 + FT], sro[0:1, 1024:1024 + FT]), inc=DV)
        p.wait("sync", t_iv2)
        p.wait("sync", t_bcB_free)
        t_sB1 = p.dma("sync", "dB", bc_d[0:1, 0:FT], sro[0:1, 0:FT])
        t_sB2 = p.dma("sync", "dB", bc_d[1:2, 0:FT], sro[0:1, 1024:1024 + FT])
        p.fence("sync", "dB")
        t_bB1 = p.dma("sync", "dB", bcB[:, 0, :], bc_d[0:1, 0:FT].broadcast_to([128, FT]))
        t_bB2 = p.dma("sync", "dB", bcB[:, 1, :], bc_d[1:2, 0:FT].broadcast_to([128, FT]))
        p.wait("vector", t_bB2)
        p.sop("vector", lambda e: e.tensor_tensor(
            sq2[:, :, :], x1b[:, :, :], bcB[:, 0:1, :].broadcast_to([128, 16, FT]), ALU.add))
        t_h2v = p.op("vector", lambda e: e.tensor_tensor(
            sq2[:, :, :], sq2[:, :, :], bcB[:, 1:2, :].broadcast_to([128, 16, FT]), ALU.mult), inc=DV)
        t_bcB_free = t_h2v
        p.wait("scalar", t_h2v)
        t_h2 = None
        for kc in range(16):
            t_h2 = p.op("scalar", lambda e, kc=kc: e.activation(
                h2b[:, kc, :], sq2[:, kc, :], AF.Identity,
                bias=modc[:, kc, 3:4], scale=modc[:, kc, 7:8]), inc=AC if kc == 15 else None)
        p.wait("sync", t_h2)
        t_h2s = p.dma("sync", "dSP", h2_d[:, tok0:tok0 + FT].rearrange("(c p) t -> p c t", p=128), h2b[:, :, :])
        t_x1_store.append(t_h2s)
        p.wait("tensor", t_h2)
        p.wait("scalar", t_h2)
        p.bank("b2")
        t_r2m = None
        for kc in range(16):
            t_sq = p.op("scalar", lambda e, kc=kc: e.activation(
                sq2[:, kc, :], h2b[:, kc, :], AF.Square), inc=AC)
            p.wait("tensor", t_sq)
            t_r2m = p.op("tensor", lambda e, kc=kc: e.matmul(
                pb[2][0:1, 0:FT], posc[:, 0:1], sq2[:, kc, :], start=(kc == 0), stop=(kc == 15)),
                inc=PE if kc == 15 else None)
        p.wait("vector", t_r2m)
        t_rv = p.op("vector", lambda e: e.tensor_scalar(sro[0:1, 0:FT], pb[2][0:1, 0:FT], EPS, None, ALU.add), inc=DV)
        p.wait("vector", t_rv)
        p.bank_done("b2", t_rv)
        p.wait("scalar", t_rv)
        t_rq2 = p.op("scalar", lambda e: e.activation(sro[0:1, 0:FT], sro[0:1, 0:FT], AF.Sqrt), inc=AC)
        p.wait("vector", t_rq2)
        t_rsq = p.op("vector", lambda e: e.reciprocal(sro[0:1, 0:FT], sro[0:1, 0:FT]), inc=DV)
        p.wait("vector", t_rsq)
        t_rm0 = p.op("vector", lambda e: e.memset(Rmx[:, :], 0.0), inc=DV)
        p.wait("vector", t_rm0)
        p.wait("scalar", t_h2)
        for kc in range(16):
            t_abk = p.op("scalar", lambda e, kc=kc: e.activation(
                sq2[:, kc, :], h2b[:, kc, :], AF.Abs, scale=fnorm_c[:, kc:kc + 1]), inc=AC)
            p.wait("vector", t_abk)
            p.sop("vector", lambda e, kc=kc: e.tensor_tensor(
                Rmx[:, :], Rmx[:, :], sq2[:, kc, :], ALU.max))
        t_Rm = p.op("vector", lambda e: e.engine_nop(), inc=DV)
        PQ = min(128, FT)
        NTQ = FT // PQ
        p.wait("tensor", t_Rm)
        t_trx = []
        for j in range(NTQ):
            p.bank("b4")
            t_tr = p.op("tensor", lambda e, j=j, PQ=PQ: e.transpose(
                pb[4][0:PQ, 0:128], Rmx[:, j * PQ:(j + 1) * PQ], identf[:, :]), inc=PE)
            p.wait("vector", t_tr)
            t_am = p.op("vector", lambda e, j=j, PQ=PQ: e.reduce_max(
                qtm[0:PQ, j:j + 1], pb[4][0:PQ, 0:128], axis=AX.X), inc=DV)
            p.bank_done("b4", t_am)
            t_trx.append(t_am)
        p.wait("sync", t_rsq)
        t_qs1 = p.dma("sync", "dQ", qs_d[0:1, 0:FT], sro[0:1, 0:FT])
        p.fence("sync", "dQ")
        t_qs2 = p.dma("sync", "dQ", qtm[0:PQ, 4:4 + NTQ], qs_d[0:1, 0:FT].rearrange("o (j p) -> (o p) j", p=PQ), allow_slow_non_contiguous=True)
        p.wait("vector", t_qs2)
        for t in t_trx:
            p.wait("vector", t)
        p.sop("vector", lambda e, PQ=PQ: e.tensor_tensor(qtm[0:PQ, 0:NTQ], qtm[0:PQ, 0:NTQ], qtm[0:PQ, 4:4 + NTQ], ALU.mult))
        p.sop("vector", lambda e, PQ=PQ: e.tensor_scalar(qtm[0:PQ, 0:NTQ], qtm[0:PQ, 0:NTQ], 1e-5, None, ALU.max))
        p.sop("vector", lambda e, PQ=PQ: e.reciprocal(qtm[0:PQ, 2:2 + NTQ], qtm[0:PQ, 0:NTQ]))
        p.sop("vector", lambda e, PQ=PQ: e.tensor_tensor(qtm[0:PQ, 2:2 + NTQ], qtm[0:PQ, 2:2 + NTQ], qtm[0:PQ, 4:4 + NTQ], ALU.mult))
        p.sop("vector", lambda e, PQ=PQ: e.tensor_scalar(qtm[0:PQ, 2:2 + NTQ], qtm[0:PQ, 2:2 + NTQ], 127.0, None, ALU.mult))
        t_qc = p.op("vector", lambda e, PQ=PQ: e.scalar_tensor_tensor(
            qtm[0:PQ, 0:NTQ], qtm[0:PQ, 0:NTQ], 1.0 / 127.0, invs_c[0:PQ, 2:3].broadcast_to([PQ, NTQ]), ALU.mult, ALU.mult), inc=DV)
        p.wait("sync", t_qc)
        t_qs3 = p.dma("sync", "dQ", qs_d[1:2, 0:FT].rearrange("(o) (j p) -> (o p) j", p=PQ), qtm[0:PQ, 2:2 + NTQ], allow_slow_non_contiguous=True)
        t_qs4 = p.dma("sync", "dQ", qs_d[2:3, 0:FT].rearrange("(o) (j p) -> (o p) j", p=PQ), qtm[0:PQ, 0:NTQ], allow_slow_non_contiguous=True)
        p.fence("sync", "dQ")
        t_qb1 = p.dma("sync", "dQ", qbc[:, 0, :], qs_d[1:2, 0:FT].broadcast_to([128, FT]))
        t_qb2 = p.dma("sync", "dQ", qbc[:, 1, :], qs_d[2:3, 0:FT].broadcast_to([128, FT]))
        p.wait("vector", t_qb2)
        t_aq = None
        for kc in range(16):
            p.op("vector", lambda e, kc=kc: e.scalar_tensor_tensor(
                sq2[:, kc, :], h2b[:, kc, :], fnorm_c[:, kc:kc + 1], qbc[:, 0, :], ALU.mult, ALU.mult),
                inc=DV if kc == 15 else None)
        p.wait("vector", (DV, p.counts[DV]))
        for kc in range(16):
            p.op("vector", lambda e, kc=kc: e.tensor_scalar(
                sq2[:, kc, :], sq2[:, kc, :], RC, -RC, ALU.add, ALU.add),
                inc=DV if kc == 15 else None)
        p.wait("vector", (DV, p.counts[DV]))
        for kc in range(16):
            t_aq = p.op("vector", lambda e, kc=kc, tok0=tok0: e.tensor_tensor(
                aqf[:, kc, tok0:tok0 + FT], sq2[:, kc, :], qbc[:, 1, :], ALU.mult), inc=DV)
        t_aq_done.append(t_aq)
        t_x1b_free = [t_x1s, t_h2v, t_q2m, t_h2s]
    ABF.pop()
    AFA.pop()
    p.barrier()

    AFA.push()
    ABF.push()
    wstr = [ABF.alloc([128, 16, 128], f"wstr{i}") for i in range(4)]
    vstage = ABF.alloc([128, NT4, T4], "vstage")
    gstage = ABF.alloc([128, NT4, T4], "gstage")
    mstage = ABF.alloc([128, NT4, T4], "mstage")
    sq3 = AFA.alloc([128, 512], "sq3")
    qtm2 = AFA.alloc([128, 2 * max(1, LSL // 128)], "qtm2")

    t_r20 = p.op("vector", lambda e: e.memset(R2x[:, :], 0.0), inc=DV)
    p.wait("vector", t_r20)
    t_w_free = [t_ag2, t_ag2, t_ag2, t_ag2]
    t_mid_st = []
    t_aq_all = t_aq_done[-1]
    p.wait("tensor", t_aq_all)
    t_vfree = None
    t_mfree = None
    t_R2 = None
    t_load = {}

    def gv_load(j2):
        wti2 = (2 * j2) % 4
        wtg2 = (2 * j2 + 1) % 4
        tls = []
        for hc, wb in ((j2, wti2), (NOC + j2, wtg2)):
            rr, lc = hc // 16, hc % 16
            p.wait("scalar", t_w_free[wb])
            tls.append(p.dma("scalar", f"lG{wb}", wstr[wb][:, :, :],
                       cc2_out[rr, 0:GVSH, lc * 128:(lc + 1) * 128].rearrange("(c p) h -> p c h", p=128)))
        t_load[j2] = tls

    gv_load(0)
    for j in range(NOC):
        wti = (2 * j) % 4
        wtg = (2 * j + 1) % 4
        if j + 1 < NOC:
            gv_load(j + 1)
        t_lv, t_lg = t_load[j]
        p.wait("tensor", t_lv)
        t_vmm = [None] * NT4
        for t4 in range(NT4):
            bk = t4 % 2
            p.bank(f"b{bk}")
            for kc in range(16):
                t_vmm[t4] = p.op("tensor", lambda e, kc=kc, t4=t4, wti=wti, bk=bk: e.matmul(
                    pb[bk][0:128, 0:T4], wstr[wti][:, kc, :], aqf[:, kc, t4 * T4:(t4 + 1) * T4],
                    start=(kc == 0), stop=(kc == 15)), inc=PE if kc == 15 else None)
            p.wait("scalar", t_vmm[t4])
            p.wait("scalar", t_vfree)
            t_ve = p.op("scalar", lambda e, t4=t4, bk=bk: e.activation(
                vstage[:, t4, :], pb[bk][0:128, 0:T4], AF.Identity), inc=AC)
            p.bank_done(f"b{bk}", t_ve)
        t_w_free[wti] = t_vmm[NT4 - 1]
        p.wait("tensor", t_lg)
        t_gmm = None
        t_ge = None
        for t4 in range(NT4):
            bk = 2 + t4 % 2
            p.bank(f"b{bk}")
            for kc in range(16):
                t_gmm = p.op("tensor", lambda e, kc=kc, t4=t4, wtg=wtg, bk=bk: e.matmul(
                    pb[bk][0:128, 0:T4], wstr[wtg][:, kc, :], aqf[:, kc, t4 * T4:(t4 + 1) * T4],
                    start=(kc == 0), stop=(kc == 15)), inc=PE if kc == 15 else None)
            p.wait("scalar", t_gmm)
            p.wait("scalar", t_vfree)
            t_ge = p.op("scalar", lambda e, t4=t4, bk=bk: e.activation(
                gstage[:, t4, :], pb[bk][0:128, 0:T4], AF.Silu), inc=AC)
            p.bank_done(f"b{bk}", t_ge)
        t_w_free[wtg] = t_gmm
        p.wait("vector", t_ve)
        p.wait("vector", t_ge)
        p.wait("vector", t_mfree)
        t_mid = p.op("vector", lambda e: e.tensor_tensor(
            mstage[:, :, :], vstage[:, :, :], gstage[:, :, :], ALU.mult), inc=DV)
        p.wait("scalar", t_mid)
        t_mab = p.op("scalar", lambda e: e.activation(
            gstage[:, :, :], mstage[:, :, :], AF.Abs), inc=AC)
        p.wait("vector", t_mab)
        t_R2 = p.op("vector", lambda e: e.tensor_tensor(
            R2x[:, :], R2x[:, :], gstage[:, :, :].rearrange("p a b -> p (a b)"), ALU.max), inc=DV)
        p.wait("vector", t_R2)
        t_vfree = t_R2
        p.wait("sync", t_mid)
        t_ms = p.dma("sync", "dM", mid_d[j * 128:(j + 1) * 128, :], mstage[:, :, :].rearrange("p a b -> p (a b)"))
        t_mfree = t_ms
        t_mid_st.append(t_ms)

    PQ2 = min(128, LSL)
    NTQ2 = LSL // PQ2
    t_am2 = []
    p.wait("tensor", t_R2)
    for jj in range(NTQ2):
        p.bank("b4")
        t_tr = p.op("tensor", lambda e, jj=jj, PQ2=PQ2: e.transpose(
            pb[4][0:PQ2, 0:128], R2x[:, jj * PQ2:(jj + 1) * PQ2], identf[:, :]), inc=PE)
        p.wait("vector", t_tr)
        t_am = p.op("vector", lambda e, jj=jj, PQ2=PQ2: e.reduce_max(
            qtm2[0:PQ2, jj:jj + 1], pb[4][0:PQ2, 0:128], axis=AX.X), inc=DV)
        p.bank_done("b4", t_am)
        t_am2.append(t_am)
    for t in t_am2:
        p.wait("vector", t)
    p.sop("vector", lambda e, PQ2=PQ2: e.tensor_scalar(qtm2[0:PQ2, 0:NTQ2], qtm2[0:PQ2, 0:NTQ2], 1e-5, None, ALU.max))
    p.sop("vector", lambda e, PQ2=PQ2: e.reciprocal(qtm2[0:PQ2, NTQ2:2 * NTQ2], qtm2[0:PQ2, 0:NTQ2]))
    p.sop("vector", lambda e, PQ2=PQ2: e.tensor_scalar(qtm2[0:PQ2, NTQ2:2 * NTQ2], qtm2[0:PQ2, NTQ2:2 * NTQ2], 127.0, None, ALU.mult))
    t_q2c = p.op("vector", lambda e, PQ2=PQ2: e.scalar_tensor_tensor(
        qtm2[0:PQ2, 0:NTQ2], qtm2[0:PQ2, 0:NTQ2], 1.0 / 127.0, invs_c[0:PQ2, 3:4].broadcast_to([PQ2, NTQ2]),
        ALU.mult, ALU.mult), inc=DV)
    p.wait("sync", t_q2c)
    t_qq1 = p.dma("sync", "dQ", qs_d[1:2, 0:LSL].rearrange("(o) (j p) -> (o p) j", p=PQ2), qtm2[0:PQ2, NTQ2:2 * NTQ2], allow_slow_non_contiguous=True)
    t_qq2 = p.dma("sync", "dQ", qs_d[2:3, 0:LSL].rearrange("(o) (j p) -> (o p) j", p=PQ2), qtm2[0:PQ2, 0:NTQ2], allow_slow_non_contiguous=True)
    p.fence("sync", "dQ")
    t_qb21 = p.dma("sync", "dQ", qbc2[:, 0, :], qs_d[1:2, 0:LSL].broadcast_to([128, LSL]))
    t_qb22 = p.dma("sync", "dQ", qbc2[:, 1, :], qs_d[2:3, 0:LSL].broadcast_to([128, LSL]))
    ABF.pop()
    AFA.pop()
    ABF.pop()
    p.barrier()

    AFA.push()
    ABF.push()
    mq = ABF.alloc([128, 64, T4], "mq")
    h2t = ABF.alloc([128, 16, T4], "h2t")
    x1t = ABF.alloc([128, 16, T4], "x1t")
    wstr2 = [ABF.alloc([128, 64, 128], f"ws2_{i}") for i in range(2)]
    sq4 = AFA.alloc([128, T4], "sq4")
    outst = AFA.alloc([128, T4], "outst")

    t_wfree2 = [None, None]
    t_o2_prev = None
    t_mq_free = None
    t_h2t_free = None
    t_st_last = None
    for t4 in range(NT4):
        ts0 = t4 * T4
        for t in t_mid_st:
            p.wait("sync", t)
        p.wait("sync", t_mq_free)
        t_ml = p.dma("sync", "dOM", mq[:, :, :],
                     mid_d[:, ts0:ts0 + T4].rearrange("(c p) t -> p c t", p=128))
        p.wait("sync", t_x1_store[-1])
        p.wait("sync", t_h2t_free)
        t_hl = p.dma("sync", "dOH", h2t[:, :, :], h2_d[:, ts0:ts0 + T4].rearrange("(c p) t -> p c t", p=128))
        t_xl = p.dma("sync", "dOH", x1t[:, :, :], x1_d[:, ts0:ts0 + T4].rearrange("(c p) t -> p c t", p=128))
        p.wait("vector", t_ml)
        p.wait("vector", t_qb21)
        p.wait("vector", t_qb22)
        t_mq2 = None
        for kc2 in range(64):
            p.sop("vector", lambda e, kc2=kc2, ts0=ts0: e.tensor_tensor(
                sq4[:, :], mq[:, kc2, :], qbc2[:, 0, ts0:ts0 + T4], ALU.mult))
            t_mq2 = p.op("vector", lambda e, kc2=kc2: e.tensor_scalar(
                mq[:, kc2, :], sq4[:, :], RC, -RC, ALU.add, ALU.add), inc=DV)
            p.wait("vector", t_mq2)
        for oc in range(16):
            bk = oc % 2
            wb = wstr2[bk]
            p.wait("scalar", t_wfree2[bk])
            t_lws = []
            for rr in range(8):
                t_lw = p.dma("scalar", f"lO{bk}", wb[:, rr * 8:(rr + 1) * 8, :],
                             cc2_out[rr, GVSH:GVSH + OUTSH, oc * 128:(oc + 1) * 128]
                             .rearrange("(c p) h -> p c h", p=128))
                t_lws.append(t_lw)
            p.wait("tensor", t_lws[-1])
            p.wait("tensor", t_mq2)
            p.bank(f"b{bk}")
            t_omm2 = None
            for kc2 in range(64):
                t_omm2 = p.op("tensor", lambda e, kc2=kc2, oc=oc, wb=wb, bk=bk: e.matmul(
                    pb[bk][0:128, 0:T4], wb[:, kc2, :], mq[:, kc2, :],
                    start=(kc2 == 0), stop=(kc2 == 63)), inc=PE if kc2 == 63 else None)
            t_wfree2[bk] = t_omm2
            p.wait("vector", t_omm2)
            p.wait("vector", t_xl)
            p.wait("vector", t_o2_prev)
            p.sop("vector", lambda e, bk=bk, ts0=ts0: e.tensor_tensor(
                sq4[:, :], pb[bk][0:128, 0:T4], qbc2[:, 1, ts0:ts0 + T4], ALU.mult))
            p.sop("vector", lambda e, oc=oc: e.tensor_tensor(
                sq4[:, :], sq4[:, :], h2t[:, oc, :], ALU.add))
            p.wait("vector", t_st_last)
            t_o2 = p.op("vector", lambda e, oc=oc: e.scalar_tensor_tensor(
                outst[:, :], sq4[:, :], modc[:, oc, 5:6], x1t[:, oc, :], ALU.mult, ALU.add), inc=DV)
            t_o2_prev = t_o2
            p.bank_done(f"b{bk}", t_o2)
            p.wait("sync", t_o2)
            t_st_last = p.dma("sync", ST, out_x[oc * 128:(oc + 1) * 128, ts0:ts0 + T4], outst[:, :])
        t_mq_free = t_omm2
        t_h2t_free = t_st_last
    ABF.pop()
    AFA.pop()

    p.wait("sync", (ST, p.counts[ST]))

    for name in p.semnames:
        p.handles[name] = ctx.enter_context(nc.semaphore(name))
    with nc.Block() as block:
        @block.sync
        def _(eng):
            for f in p.ops["sync"]:
                f(eng)

        @block.gpsimd
        def _(eng):
            for f in p.ops["gpsimd"]:
                f(eng)

        @block.vector
        def _(eng):
            for f in p.ops["vector"]:
                f(eng)

        @block.scalar
        def _(eng):
            for f in p.ops["scalar"]:
                f(eng)

        @block.tensor
        def _(eng):
            for f in p.ops["tensor"]:
                f(eng)
    ctx.close()
    return nc, dbg


def make_in_maps(inputs, L=8192):
    x = np.asarray(inputs["L0_attn_x"])
    Wq = np.asarray(inputs["L0_attn_Wq"])
    Wk = np.asarray(inputs["L0_attn_Wk"])
    Wv = np.asarray(inputs["L0_attn_Wv"])
    Wb = np.asarray(inputs["L0_attn_Wb"])
    Wa = np.asarray(inputs["L0_attn_Wa"])
    Wg = np.asarray(inputs["L0_attn_Wg"])
    Wo = np.asarray(inputs["L0_attn_Wo"])
    A_log = np.asarray(inputs["L0_attn_A_log"])
    dt_bias = np.asarray(inputs["L0_attn_dt_bias"])
    conv_q = np.asarray(inputs["L0_attn_conv_q"])
    conv_k = np.asarray(inputs["L0_attn_conv_k"])
    conv_v = np.asarray(inputs["L0_attn_conv_v"])
    o_norm_w = np.asarray(inputs["L0_attn_o_norm_w"])
    ffn_norm = np.asarray(inputs["L0_ffn_norm_w"])
    w_gv = np.asarray(inputs["L0_ffn_w_gv"])
    w_out = np.asarray(inputs["L0_ffn_w_out"])
    c = np.asarray(inputs["L0_c"])
    wmod = np.asarray(inputs["L0_wmod"])
    LSL = L // 4
    MODSH = 12288 // 8

    ac = np.ascontiguousarray
    in_maps = []
    for r in range(8):
        b, hg = r // 4, r % 4
        heads = [2 * hg, 2 * hg + 1]
        xTb = ac(x[b].T)
        cols = []
        for h in heads:
            cols.append(Wq[h * 128:(h + 1) * 128, :].T)
        for sg in range(2):
            for h in heads:
                cols.append(Wk[sg * 1024 + h * 128:sg * 1024 + (h + 1) * 128, :].T)
        for sg in range(2):
            for h in heads:
                cols.append(Wv[sg * 1024 + h * 128:sg * 1024 + (h + 1) * 128, :].T)
        for h in heads:
            cols.append(Wg[h * 128:(h + 1) * 128, :].T)
        brow = []
        for sg in range(2):
            for h in heads:
                brow.append(Wb[sg * 8 + h:sg * 8 + h + 1, :].T)
        cols.extend(brow)
        cols.append(Wa[heads[0]:heads[0] + 1, :].T)
        cols.append(Wa[heads[1]:heads[1] + 1, :].T)
        wproj = ac(np.concatenate(cols, axis=1))
        crow = [conv_q[heads[0] * 128:(heads[0] + 1) * 128], conv_q[heads[1] * 128:(heads[1] + 1) * 128]]
        for sg in range(2):
            for h in heads:
                crow.append(conv_k[sg * 1024 + h * 128:sg * 1024 + (h + 1) * 128])
        for sg in range(2):
            for h in heads:
                crow.append(conv_v[sg * 1024 + h * 128:sg * 1024 + (h + 1) * 128])
        convw = ac(np.concatenate(crow, axis=0))
        in_maps.append({
            "xT": xTb,
            "xsl": ac(xTb[:, hg * LSL:(hg + 1) * LSL]),
            "wproj": wproj,
            "convw": convw,
            "wo_t": None,
            "wo_in": ac(Wo.T),
            "wmod_sl": ac(wmod[:, r * MODSH:(r + 1) * MODSH]),
            "wgv_sh": ac(w_gv[r * 2048:(r + 1) * 2048, :].T),
            "wout_sh": ac(w_out.T[r * 1024:(r + 1) * 1024, :]),
            "cT": ac(c.T),
            "alog": ac(A_log[heads].reshape(2, 1)),
            "dtbi": ac(dt_bias[heads].reshape(2, 1)),
            "onorm": ac(o_norm_w.reshape(1, 128)),
            "fnorm": ac(ffn_norm.reshape(16, 128).T),
        })
        del in_maps[-1]["wo_t"]
    return in_maps


_NC_CACHE = {}


def kernel(**inputs):
    L = int(np.asarray(inputs["L0_attn_x"]).shape[1])
    if L not in _NC_CACHE:
        _NC_CACHE[L] = build_program(L)
    nc, _ = _NC_CACHE[L]
    in_maps = make_in_maps(inputs, L)
    res = run_bass_kernel_spmd(nc, in_maps, core_ids=list(range(8)))
    LSL = L // 4
    out = np.empty((2, L, 2048), dtype=np.float32)
    for r in range(8):
        b, hg = r // 4, r % 4
        out[b, hg * LSL:(hg + 1) * LSL, :] = res.results[r]["out_x"].T
    return out
